# version 36 (final)
# baseline (speedup 1.0000x reference)
import contextlib
import numpy as np
import ml_dtypes
import concourse.bass as bass
import concourse.mybir as mybir
from concourse.bass_utils import run_bass_kernel_spmd

F32 = mybir.dt.float32
BF16 = mybir.dt.bfloat16
I32 = mybir.dt.int32
U8 = mybir.dt.uint8
ALU = mybir.AluOpType
AF = mybir.ActivationFunctionType
AX = mybir.AxisListType

S, D, H, DH = 4096, 1024, 16, 64
NT, TT, NS = 8, 512, 32
DFF = 2816
C6 = float(np.float32(1.0 / 6.0))
N_CORES = 8
SBUF_LIMIT = 212000
DEV_SERIAL_CONV = False
ENGS = ("tensor", "scalar", "vector", "gpsimd", "sync")
ENG2SEM = {"tensor": "pe", "scalar": "act", "vector": "dve", "gpsimd": "gp"}


class Arena:
    def __init__(self, ap_u8, limit):
        self.u8, self.off, self.limit = ap_u8, 0, limit

    def alloc(self, cols, dt, align=64):
        esz = {F32: 4, BF16: 2, I32: 4}[dt]
        self.off = (self.off + align - 1) // align * align
        start = self.off
        self.off += cols * esz
        assert self.off <= self.limit, f"SBUF overflow {self.off} > {self.limit}"
        return self.u8.bitcast(dt)[:, start // esz: start // esz + cols]


class Op:
    __slots__ = ("eng", "fn", "waits", "incs")

    def __init__(self, eng, fn, waits, incs):
        self.eng, self.fn, self.waits, self.incs = eng, fn, waits, incs


class Prog:

    def __init__(self):
        self.ops, self.sems = [], {}
        self.fence = []
        self.fence_seen = set()

    def add_sem(self, name, handle):
        self.sems[name] = [handle, 0]

    def do(self, eng, fn, waits=(), dma=None):
        w = list(waits)
        if eng != "sync" and eng not in self.fence_seen:
            w += self.fence
            self.fence_seen.add(eng)
        if dma is not None:
            semn, by = dma, 16
        else:
            semn, by = ENG2SEM.get(eng), 1
        wl = [(self.sems[n][0], v) for (n, v) in w if v > 0]
        if semn is None:
            self.ops.append(Op(eng, fn, wl, []))
            return None
        s = self.sems[semn]
        s[1] += by
        self.ops.append(Op(eng, fn, wl, [(s[0], by)]))
        return (semn, s[1])

    def close_stage(self):
        self.fence = [(n, self.sems[n][1]) for n in ("pe", "act", "dve", "gp")]
        self.fence_seen = set()

    def emit(self, eng_name, eng):
        wm = {}
        for op in self.ops:
            if op.eng != eng_name:
                continue
            for (h, v) in op.waits:
                k = h.num
                if wm.get(k, -1) >= v:
                    continue
                eng.wait_ge(h, v)
                wm[k] = v
            inst = op.fn(eng)
            for (h, a) in op.incs:
                assert inst is not None
                inst.then_inc(h, a)


def build_program(taps=None, n_tiles=NT, do_pc=True):
    taps = taps or {}
    nc = bass.Bass("TRN2", target_bir_lowering=False)

    x_h = nc.dram_tensor("x", [S, D], F32, kind="ExternalInput").ap()
    wqkv_h = nc.dram_tensor("wqkv", [3 * D, D], F32, kind="ExternalInput").ap()
    bqkv_h = nc.dram_tensor("bqkv", [3 * D], F32, kind="ExternalInput").ap()
    wconv_h = nc.dram_tensor("wconv", [D, D, 3], F32, kind="ExternalInput").ap()
    bconv_h = nc.dram_tensor("bconv", [D], F32, kind="ExternalInput").ap()
    wout_h = nc.dram_tensor("wout", [D, D], F32, kind="ExternalInput").ap()
    bout_h = nc.dram_tensor("bout", [D], F32, kind="ExternalInput").ap()
    wg_h = nc.dram_tensor("wg", [DFF, D], F32, kind="ExternalInput").ap()
    wu_h = nc.dram_tensor("wu", [DFF, D], F32, kind="ExternalInput").ap()
    wd_h = nc.dram_tensor("wd", [D, DFF], F32, kind="ExternalInput").ap()
    out_h = nc.dram_tensor("out", [S, D], F32, kind="ExternalOutput").ap()

    sc_wqkvT = nc.dram_tensor("sc_wqkvT", [128, 8 * 3 * D], BF16).ap()
    sc_wconvT = nc.dram_tensor("sc_wconvT", [128, 24 * D], BF16).ap()
    sc_woutT = nc.dram_tensor("sc_woutT", [128, 8 * D], BF16).ap()
    sc_wgT = nc.dram_tensor("sc_wgT", [128, 8 * DFF], BF16).ap()
    sc_wuT = nc.dram_tensor("sc_wuT", [128, 8 * DFF], BF16).ap()
    sc_wdT = nc.dram_tensor("sc_wdT", [128, 22 * D], BF16).ap()
    sc_xq = nc.dram_tensor("sc_xq", [S, D], BF16).ap()
    sc_y1 = nc.dram_tensor("sc_y1", [S, D], F32).ap()

    tap_h = {n: nc.dram_tensor("tap_" + n, sp[0], sp[1], kind="ExternalOutput").ap()
             for n, sp in taps.items()}

    P = Prog()

    with (
        nc.sbuf_tensor("mega", [128, SBUF_LIMIT], U8) as mega_t,
        nc.psum_tensor("PS8", [128, 4096], F32) as PS8_t,
        contextlib.ExitStack() as _stk,
    ):
        mega = mega_t.ap()
        PS8 = PS8_t.ap()
        PS8b = PS8.bitcast(BF16)
        SP = PS8[:, 0:2048]
        CTE = PS8[:, 3072:3584]
        CTO = PS8[:, 3584:4096]
        G0 = PS8[:, 3072:3584]
        G1 = PS8[:, 3584:4096]
        G0b = PS8b[:, 6144:7168]
        G1b = PS8b[:, 7168:8192]
        CTEb = PS8b[:, 6144:7168]
        SEM_NAMES = ["pe", "act", "dve", "gp",
                     "dwq", "pin0", "pin1", "pin2", "pinc", "pinc2",
                     "flq", "flo", "flc",
                     "lh0", "lh1", "lh2", "lh3", "rh0", "rh1",
                     "xm", "xf0", "xf1", "vb", "dy", "dxq",
                     "wqe", "wqf", "wgt", "wgu", "wgd",
                     "xs0", "xs1", "ys0", "ys1", "out"]
        for nm in SEM_NAMES:
            P.add_sem(nm, _stk.enter_context(nc.semaphore("s_" + nm)))
        block = _stk.enter_context(nc.Block())

        A = Arena(mega, SBUF_LIMIT)
        KT = A.alloc(8 * S, BF16)
        V = A.alloc(NS * 1040, BF16)
        CTXW = A.alloc(2 * 8 * 514, BF16)
        PBUF = A.alloc(2 * 2048, BF16)
        Q1 = A.alloc(8 * TT, BF16)
        XT = A.alloc(8 * TT, BF16)
        CTX2 = XT
        XTM = A.alloc(1024, BF16)
        XF = A.alloc(2 * 1024, F32)
        LHSP = A.alloc(4 * 1024, BF16)
        RHSP = A.alloc(2 * 512, BF16)
        XQO = A.alloc(1024, BF16)
        XQT12 = A.alloc(2 * 1024, BF16)
        XQT1 = XQT12[:, 0:1024]
        XQT2 = XQT12[:, 1024:2048]
        BCS = A.alloc(1024, BF16)
        TMPO = A.alloc(512, BF16)
        ONESB = A.alloc(128, BF16)
        ONESF = A.alloc(128, F32)
        IDENTF = A.alloc(128, F32)
        TRI = A.alloc(128, BF16)
        WQF = A.alloc(256, F32)
        WQU = A.alloc(256, F32)
        WQB = A.alloc(2 * 256, BF16)
        WQO = A.alloc(256, BF16)
        WQT = A.alloc(256, BF16)
        WQS = A.alloc(48, F32)
        pb_dead_top = A.off
        IDENT = A.alloc(128, BF16)
        SM = A.alloc(384, F32)
        pb_top = A.off

        SM_BQKV = SM[:, 0:24]
        SM_BCONV = SM[:, 24:32]
        SM_QA = SM[0:1, 32:33]
        SM_KA = SM[0:1, 33:34]
        SM_T0 = SM[0:1, 34:35]
        SM_CV = SM[0:1, 35:36]
        SM_EB = SM[:, 36:37]
        SM_EPS = SM[:, 37:38]
        SM_C64 = SM[:, 38:39]
        SM_C128 = SM[:, 39:40]
        SM_R0 = SM[:, 40:41]
        SM_R1 = SM[:, 41:42]
        SM_R2 = SM[:, 42:43]
        SM_R3 = SM[:, 43:44]
        SM_R4 = SM[:, 44:45]
        SM_ZERO = SM[:, 46:47]
        SM_QAP = SM[:, 49:50]
        SM_KAP = SM[:, 50:51]
        SM_TAP = SM[:, 51:52]
        LNSTG = XQT12.bitcast(F32)[0:1, 0:1024]
        SM_RED = SM[:, 60:61]
        SM_RED2 = SM[0:1, 62:63]
        SM_XB = SM[:, 64:128]
        SM_XS = SM[:, 128:192]
        SM_XR = SM[:, 192:256]
        WQ_B = WQS[:, 0:16]
        WQ_S = WQS[:, 16:32]
        WQ_R = WQS[:, 32:48]

        A2 = Arena(mega, SBUF_LIMIT)
        WG = A2.alloc(8 * DFF, BF16)
        WU = A2.alloc(8 * DFF, BF16)
        WD = A2.alloc(22 * D, BF16)
        XQS = A2.alloc(2 * 1024, BF16)
        XQTT = A2.alloc(8 * TT, BF16)
        Y1S = A2.alloc(2 * 1024, BF16)
        SG = A2.alloc(DFF, BF16)
        HF = A2.alloc(DFF, F32)
        HQ = A2.alloc(DFF, BF16)
        HQT = A2.alloc(22 * 128, BF16)
        HQT2 = A2.alloc(DFF, BF16)
        SG2 = A2.alloc(DFF, BF16)
        HQ2 = A2.alloc(DFF, BF16)
        OUTF = A2.alloc(2 * 1024, F32)
        SM2 = A2.alloc(640, F32)
        S2B = A2.alloc(176, F32)
        assert A2.off <= pb_dead_top, (A2.off, pb_dead_top)
        S2_R0 = SM2[:, 0:1]
        S2_R1 = SM2[:, 1:2]
        S2_R2 = SM2[:, 2:3]
        S2_R3 = SM2[:, 3:4]
        S2_R4 = SM2[:, 4:5]
        S2_HB = SM2[:, 64:240]
        S2_HS = SM2[:, 240:416]
        S2_HR = SM2[:, 416:592]

        def lp(call):
            with nc.allow_low_precision(reason="bf16 softmax normalizer (validated)"):
                return call()

        def tap(name, src, waits=None):
            if name in tap_h:
                P.do("sync", lambda e, s=src, d=tap_h[name]: e.dma_start(d, s),
                     waits=list(P.fence) if waits is None else waits, dma="out")

        pfree = {}

        def bank_ap(idx):
            return PS8[:, idx * 512:(idx + 1) * 512]

        def bkey(idx):
            return idx

        def wb(idx):
            m = pfree.get(idx)
            return [m] if m else []

        P.do("gpsimd", lambda e: e.memset(ONESB[:, :], 1.0))
        P.do("gpsimd", lambda e: e.memset(ONESF[:, :], 1.0))
        P.do("gpsimd", lambda e: e.memset(IDENT[:, :], 0.0))
        P.do("gpsimd", lambda e: e.affine_select(
            out=IDENT[:, :], in_=IDENT[:, :], compare_op=ALU.not_equal,
            fill=1.0, base=0, pattern=[[-1, 128]], channel_multiplier=1))
        P.do("gpsimd", lambda e: e.memset(IDENTF[:, :], 0.0))
        P.do("gpsimd", lambda e: e.affine_select(
            out=IDENTF[:, :], in_=IDENTF[:, :], compare_op=ALU.not_equal,
            fill=1.0, base=0, pattern=[[-1, 128]], channel_multiplier=1))
        P.do("gpsimd", lambda e: e.memset(TRI[:, :], 1.0))
        P.do("gpsimd", lambda e: e.affine_select(
            out=TRI[:, :], in_=TRI[:, :], compare_op=ALU.is_ge, fill=0.0,
            base=0, pattern=[[1, 128]], channel_multiplier=-1))
        P.do("gpsimd", lambda e: e.memset(
            V.rearrange("p (r h d) -> p r h d", h=H, d=65)[:, :, :, 64:65], 1.0))
        P.do("gpsimd", lambda e: e.memset(SM[0:1, 32:36], 1e-6))
        P.do("gpsimd", lambda e: e.memset(SM[:, 36:64], 0.0))
        P.do("gpsimd", lambda e: e.memset(SM_EPS, 1e-5))
        P.do("gpsimd", lambda e: e.memset(SM_C64, 64.0))
        P.do("gpsimd", lambda e: e.memset(SM_C128, 128.0))
        P.close_stage()

        P.do("sync", lambda e: e.dma_start(
            WQF[0:24, 0:128], bqkv_h.rearrange("(a b) -> a b", b=128)), dma="dwq")
        m = P.do("sync", lambda e: e.dma_start(
            WQF[0:8, 128:256], bconv_h.rearrange("(a b) -> a b", b=128)), dma="dwq")
        m2 = P.do("tensor", lambda e: e.transpose(
            G0[:, 0:24], WQF[0:24, 0:128], IDENTF[0:24, 0:24]), waits=[m])
        mc_ = P.do("vector", lambda e: e.tensor_copy(SM_BQKV, G0[:, 0:24]), waits=[m2])
        m2 = P.do("tensor", lambda e: e.transpose(
            G0[:, 24:32], WQF[0:8, 128:256], IDENTF[0:8, 0:8]), waits=[m, mc_])
        P.do("vector", lambda e: e.tensor_copy(SM_BCONV, G0[:, 24:32]), waits=[m2])
        P.close_stage()

        PIN = [HF.bitcast(BF16)[:, 0:1024], HF.bitcast(BF16)[:, 1024:2048],
               KT[:, 0:1024]]
        PINC = [XQTT[:, 0:3072], KT[:, 1024:4096]]
        PSTG = [OUTF.bitcast(BF16)[:, 0:1024], OUTF.bitcast(BF16)[:, 1024:2048]]
        MK = {}
        shared = {}

        def cvt_plain(src_h, dst_sc, n_strips, outw, tag, flsem):
            last = {}
            mfin = None
            for st in range(n_strips):
                b = st % 2
                lb = st % 3
                w = [last[("pe", st - 3)]] if ("pe", st - 3) in last else []
                src = src_h.rearrange("(s p) c -> s p c", p=128)[st]
                md = P.do("gpsimd", lambda e, s_=src, lb_=lb: e.dma_start(
                    PIN[lb_], s_), waits=w, dma="pin" + str(lb))
                pm = None
                for i in range(8):
                    wz = [md]
                    if i == 0 and ("ev", st - 2) in last:
                        wz.append(last[("ev", st - 2)])
                    pm = P.do("tensor", lambda e, i_=i, b_=b, lb_=lb: e.transpose(
                        [G0b, G1b][b_].rearrange("p (i c) -> p i c", c=128)[:, i_],
                        PIN[lb_].rearrange("p (i c) -> p i c", c=128)[:, i_],
                        IDENT[:, :]), waits=wz)
                last[("pe", st)] = pm
                wfl = [last[("fl", st - 2)]] if ("fl", st - 2) in last else []
                me = P.do("vector", lambda e, b_=b: e.tensor_copy(
                    PSTG[b_], [G0b, G1b][b_][:, 0:1024]), waits=[pm] + wfl)
                last[("ev", st)] = me
                mfin = P.do("gpsimd", lambda e, b_=b, st_=st: e.dma_start(
                    dst_sc.rearrange("p (i o) -> p i o", o=outw)
                    [:, :, st_ * 128: st_ * 128 + 128],
                    PSTG[b_].rearrange("p (i c) -> p i c", c=128)),
                    waits=[me], dma=flsem)
                last[("fl", st)] = mfin
            return mfin

        MK["wqkvT"] = cvt_plain(wqkv_h, sc_wqkvT, 24, 3 * D, "qkv", "flq")
        MK["woutT"] = cvt_plain(wout_h, sc_woutT, 8, D, "out", "flo")

        def cvt_conv():
            mfin = None
            prev_ev = None
            prev_flush = None
            pelast = {}
            for st in range(8):
                lb = st % 2
                src = wconv_h.rearrange("(s p) c k -> s p (c k)", p=128)[st]
                w = [pelast[st - 2]] if st >= 2 else []
                md = P.do("gpsimd", lambda e, s_=src, lb_=lb: e.dma_start(
                    PINC[lb_], s_), waits=w,
                    dma=("pinc" if lb == 0 else "pinc2"))
                for t_ in range(3):
                    pm = None
                    for i in range(8):
                        tin = PINC[lb].rearrange("p (i k) -> p i k", k=3)[:, :, t_] \
                            .rearrange("p (j c) -> p j c", c=128)[:, i]
                        wz = [md]
                        if i == 0 and prev_ev is not None:
                            wz.append(prev_ev)
                        pm = P.do("tensor", lambda e, tin_=tin, i_=i: e.transpose(
                            G0b.rearrange("p (i c) -> p i c", c=128)[:, i_],
                            tin_, IDENT[:, :]), waits=wz)
                    mprev_pe = pm
                    pelast[st] = pm
                    me = P.do("vector", lambda e: e.tensor_copy(PSTG[0], G0b[:, 0:1024]),
                              waits=[pm] + ([prev_flush] if prev_flush else []))
                    prev_ev = me
                    mfin = P.do("gpsimd", lambda e, st_=st, tt_=t_: e.dma_start(
                        sc_wconvT.rearrange("p (i o) -> p i o", o=D)
                        [:, tt_ * 8:(tt_ + 1) * 8, st_ * 128: st_ * 128 + 128],
                        PSTG[0].rearrange("p (i c) -> p i c", c=128)),
                        waits=[me], dma="flc")
                    prev_flush = mfin
            return mfin

        mprev_pe = None
        MK["wconvT"] = cvt_conv()
        MK3 = [MK["wqkvT"], MK["woutT"], MK["wconvT"]]
        for _eng in ("tensor", "scalar", "vector", "gpsimd"):
            P.do(_eng, lambda e: e.nop(), waits=MK3)
        P.close_stage()

        wq_pieces = []
        for (h_, d_) in ((wg_h, sc_wgT), (wu_h, sc_wuT)):
            for st in range(22):
                for pc_ in range(4):
                    wq_pieces.append((h_, d_, DFF, st, pc_ * 256, 256))
        for st in range(8):
            for pc_ in range(11):
                wq_pieces.append((wd_h, sc_wdT, D, st, pc_ * 256, 256))
        wqs = {"i": 0}

        def wq_quant():
            if wqs["i"] >= len(wq_pieces):
                return
            (src_h, dst_sc, outw, st, c0, L) = wq_pieces[wqs["i"]]
            wqs["i"] += 1
            nb = L // 16
            g = "vector"
            w = [wqs["q"]] if "q" in wqs else []
            md = P.do("gpsimd", lambda e, s_=src_h.rearrange("(s p) c -> s p c", p=128)[st]
                      [:, c0:c0 + L], L_=L: e.dma_start(WQF[:, 0:L_], s_),
                      waits=w, dma="wqe")
            wr = [wqs["rcp"]] if "rcp" in wqs else []
            m1 = P.do("vector", lambda e, L_=L, n_=nb: e.tensor_reduce(
                WQ_B[:, 0:n_], WQF[:, 0:L_].rearrange("p (a b) -> p a b", b=16),
                axis=AX.X, op=ALU.max, apply_absolute_value=True),
                waits=[md] + wr)
            m2 = P.do("vector", lambda e, n_=nb: e.tensor_scalar(
                WQ_S[:, 0:n_], WQ_B[:, 0:n_], C6, 1e-12, op0=ALU.mult, op1=ALU.max),
                waits=[m1])
            m3a = P.do("scalar", lambda e, n_=nb: e.activation(
                WQ_B[:, 0:n_], WQ_S[:, 0:n_], AF.Ln, bias=SM_ZERO), waits=[m2])
            m3 = P.do("scalar", lambda e, n_=nb: e.activation(
                WQ_R[:, 0:n_], WQ_B[:, 0:n_], AF.Exp, scale=-1.0), waits=[m3a])
            wqs["rcp"] = m3
            m4 = P.do(g, lambda e, L_=L, n_=nb: e.tensor_tensor(
                WQU[:, 0:L_].rearrange("p (a b) -> p a b", b=16),
                WQF[:, 0:L_].rearrange("p (a b) -> p a b", b=16),
                WQ_R[:, 0:n_].unsqueeze(2).broadcast_to([128, n_, 16]),
                op=ALU.mult), waits=[m3])
            sg1 = P.do(g, lambda e, L_=L: e.tensor_scalar(
                WQB[:, 256:256 + L_], WQU[:, 0:L_], 0.0, -2.0,
                op0=ALU.is_lt, op1=ALU.mult), waits=[m4])
            sg2 = P.do(g, lambda e, L_=L: e.tensor_scalar(
                WQB[:, 256:256 + L_], WQB[:, 256:256 + L_], 1.0, None,
                op0=ALU.add), waits=[sg1])
            m5 = P.do(g, lambda e, L_=L: e.tensor_tensor(
                WQF[:, 0:L_], WQU[:, 0:L_], WQB[:, 256:256 + L_], op=ALU.mult),
                waits=[sg2])
            f1 = P.do(g, lambda e, L_=L: e.tensor_scalar(
                WQB[:, 0:L_], WQF[:, 0:L_], 64.0, None, op0=ALU.add), waits=[m5])
            a1 = P.do(g, lambda e, L_=L: e.tensor_scalar(
                WQB[:, 0:L_], WQB[:, 0:L_], 64.0, 2.0, op0=ALU.subtract, op1=ALU.min),
                waits=[f1])
            wtp = [wqs["tpm"]] if "tpm" in wqs else []
            f2 = P.do(g, lambda e, L_=L: e.tensor_scalar(
                WQO[:, 0:L_], WQF[:, 0:L_], 128.0, None, op0=ALU.add),
                waits=[a1] + wtp)
            b1 = P.do(g, lambda e, L_=L: e.tensor_scalar(
                WQO[:, 0:L_], WQO[:, 0:L_], 130.0, 0.0, op0=ALU.subtract, op1=ALU.max),
                waits=[f2])
            g0a = P.do(g, lambda e, L_=L: e.tensor_scalar(
                WQO[:, 0:L_], WQO[:, 0:L_], 2.0, None, op0=ALU.min), waits=[b1])
            g0 = P.do(g, lambda e, L_=L: e.tensor_tensor(
                WQO[:, 0:L_], WQO[:, 0:L_], WQB[:, 0:L_], op=ALU.add), waits=[g0a])
            c1 = P.do(g, lambda e, L_=L: e.tensor_scalar(
                WQB[:, 0:L_], WQF[:, 0:L_], 5.0, 2.0, op0=ALU.is_gt, op1=ALU.mult),
                waits=[g0])
            g2 = P.do(g, lambda e, L_=L: e.tensor_tensor(
                WQO[:, 0:L_], WQO[:, 0:L_], WQB[:, 0:L_], op=ALU.add), waits=[c1])
            t1 = P.do(g, lambda e, L_=L: e.tensor_tensor(
                WQO[:, 0:L_], WQO[:, 0:L_], WQB[:, 256:256 + L_], op=ALU.mult),
                waits=[g2, sg2])
            q = P.do(g, lambda e, L_=L, n_=nb: e.tensor_tensor(
                WQO[:, 0:L_].rearrange("p (a b) -> p a b", b=16),
                WQO[:, 0:L_].rearrange("p (a b) -> p a b", b=16),
                WQ_S[:, 0:n_].unsqueeze(2).broadcast_to([128, n_, 16]),
                op=ALU.mult), waits=[t1])
            wqs["q"] = q
            wqs["pend"] = (dst_sc, outw, st, c0, L, q)

        def wq_flush(att=None):
            if "pend" not in wqs:
                return
            (dst_sc, outw, st, c0, L, q) = wqs.pop("pend")
            nblk = L // 128
            if att is None:
                tb_ap, tkey, extw = CTEb, 6, []
            else:
                rg_a, em_a = att
                tb_ap = PS8b[:, rg_a * 2048: rg_a * 2048 + 2048]
                tkey = 2 * rg_a
                extw = [em_a]
            pm = None
            for b in range(nblk):
                wz = [q]
                if b == 0:
                    wz += wb(tkey) + extw + ([wqs["tf"]] if "tf" in wqs else [])
                pm = P.do("tensor", lambda e, b_=b, a_=tb_ap: e.transpose(
                    a_.rearrange("p (i c) -> p i c", c=128)[:, b_],
                    WQO.rearrange("p (i c) -> p i c", c=128)[:, b_],
                    IDENT[:, :]), waits=wz)
            wqs["tpm"] = pm
            me = P.do("vector", lambda e, L_=L, a_=tb_ap: e.tensor_copy(
                WQT[:, 0:L_], a_[:, 0:L_]), waits=[pm])
            pfree[tkey] = me
            mf = P.do("sync", lambda e, d_=dst_sc, ow=outw, st_=st, c0_=c0, n_=nblk:
                      e.dma_start(
                          d_.rearrange("p (i o) -> p i o", o=ow)
                          [:, c0_ // 128: c0_ // 128 + n_, st_ * 128: st_ * 128 + 128],
                          WQT[:, 0:n_ * 128].rearrange("p (i c) -> p i c", c=128)),
                      waits=[me], dma="wqf")
            wqs["tf"] = mf

        lh = {"slot": 0, "free": {}}

        def lhsp_load(src_ap, waits=()):
            sl = lh["slot"]
            lh["slot"] = (sl + 1) % 4
            w = list(waits) + ([lh["free"][sl]] if sl in lh["free"] else [])
            m = P.do("sync", lambda e, s_=src_ap, sl_=sl: e.dma_start(
                LHSP.rearrange("p (s c) -> p s c", c=1024)[:, sl_], s_),
                waits=w, dma="lh" + str(sl))
            return sl, m

        rh = {"slot": 0, "free": {}}

        def rhsp_load(src_ap, waits=()):
            sl = rh["slot"]
            rh["slot"] = (sl + 1) % 2
            w = list(waits) + ([rh["free"][sl]] if sl in rh["free"] else [])
            m = P.do("sync", lambda e, s_=src_ap, sl_=sl: e.dma_start(
                RHSP.rearrange("p (s c) -> p s c", c=512)[:, sl_], s_),
                waits=w, dma="rh" + str(sl))
            return sl, m

        def emit_xt_qkv(c):
            xst = {}
            mtr = []
            q1g = [attn_state["qlast"]] if "qlast" in attn_state else []
            B01 = [PS8b[:, 0:1024], PS8b[:, 1024:2048]]
            for r in range(4):
                w = [xst[r - 1]] if (r - 1) in xst else (
                    [shared["xtmlast"]] if "xtmlast" in shared else [])
                md = P.do("gpsimd", lambda e, r_=r: e.dma_start(
                    XTM[:, 0:1024],
                    x_h.rearrange("(s p) c -> s p c", p=128)[4 * c + r_]),
                    waits=w, dma="xm")
                pm = None
                for i in range(8):
                    wz = [md] + (wb(r % 2) if i == 0 else [])
                    pm = P.do("tensor", lambda e, i_=i, a_=B01[r % 2]: e.transpose(
                        a_.rearrange("p (i c) -> p i c", c=128)[:, i_],
                        XTM[:, 0:1024].rearrange("p (i c) -> p i c", c=128)[:, i_],
                        IDENT[:, :]), waits=wz)
                xst[r] = pm
                me = P.do("vector", lambda e, r_=r, a_=B01[r % 2]: e.tensor_copy(
                    XT.rearrange("p (i c) -> p i c", c=512)[:, :, r_ * 128:(r_ + 1) * 128],
                    a_.rearrange("p (i c) -> p i c", c=128)),
                    waits=[pm])
                pfree[bkey(r % 2)] = me
                mtr.append(me)
            for o in range(16):
                sl, mw = lhsp_load(
                    sc_wqkvT.rearrange("p (i o) -> p i o", o=3 * D)
                    [:, :, o * 128:(o + 1) * 128], waits=MK3)
                bi = 4 + (o % 2)
                pm = None
                for i in range(8):
                    wz = [mw]
                    if i == 0:
                        wz += wb(bi) + mtr
                    pm = P.do("tensor", lambda e, i_=i, sl_=sl, bi_=bi: e.matmul(
                        bank_ap(bi_),
                        LHSP.rearrange("p (s c) -> p s c", c=1024)[:, sl_]
                        .rearrange("p (i c) -> p i c", c=128)[:, i_],
                        XT[:, i_ * 512:(i_ + 1) * 512],
                        start=(i_ == 0), stop=(i_ == 7), skip_group_check=True),
                        waits=wz)
                lh["free"][sl] = pm
                if o < 8:
                    dst = Q1[:, o * 512:(o + 1) * 512]
                else:
                    dst = KT[:, (o - 8) * S + c * 512: (o - 8) * S + c * 512 + 512]
                me = P.do("scalar", lambda e, d_=dst, bi_=bi, o_=o: e.activation(
                    d_, bank_ap(bi_), AF.Identity, bias=SM_BQKV[:, o_:o_ + 1]),
                    waits=[pm] + q1g)
                pfree[bkey(bi)] = me
                shared["qkvevac"] = me
                mar = P.do("vector", lambda e, d_=dst: e.tensor_reduce(
                    SM_RED, d_, axis=AX.X, op=ALU.max,
                    apply_absolute_value=True),
                    waits=[me] + ([shared[("accrd", 1)]]
                                  if o == 0 and ("accrd", 1) in shared else []))
                mac = P.do("vector", lambda e, t_=(SM_QAP if o < 8 else SM_KAP):
                           e.tensor_tensor(t_, t_, SM_RED, op=ALU.max),
                           waits=[mar])
                shared["qaccl"] = mac
                if c == 0 and o == 0:
                    tap("lhs0", LHSP[:, 0:1024], waits=[pm])
            for hv in range(2):
                first = [True] * 4
                for i in range(8):
                    sl, mw = rhsp_load(
                        sc_wqkvT.rearrange("p (i o) -> p i o", o=3 * D)
                        [:, i, 2048 + hv * 512: 2048 + hv * 512 + 512],
                        waits=MK3)
                    for r in range(4):
                        wz = [mw]
                        if first[r]:
                            wz += wb(r) + [mtr[r]]
                            first[r] = False
                        pm = P.do("tensor", lambda e, i_=i, sl_=sl, r_=r: e.matmul(
                            bank_ap(r_),
                            XT[:, i_ * 512 + r_ * 128: i_ * 512 + r_ * 128 + 128],
                            RHSP.rearrange("p (s c) -> p s c", c=512)[:, sl_],
                            start=(i_ == 0), stop=False, skip_group_check=True),
                            waits=wz)
                        rh["free"][sl] = pm
                mw = P.do("gpsimd", lambda e, hv_=hv: e.dma_start(
                    BCS[0:1, hv_ * 512:(hv_ + 1) * 512],
                    bqkv_h.rearrange("(a c) -> a c", c=512)[4 + hv_: 5 + hv_, :]),
                    waits=([shared["bcslast"]] if "bcslast" in shared else []),
                    dma="vb")
                for r in range(4):
                    pm = P.do("tensor", lambda e, hv_=hv, r_=r: e.matmul(
                        bank_ap(r_), ONESB[0:1, 0:128],
                        BCS[0:1, hv_ * 512:(hv_ + 1) * 512],
                        start=False, stop=True, skip_group_check=True),
                        waits=[mw])
                    me = P.do("vector", lambda e, r_=r, hv_=hv, c_=c: e.tensor_copy(
                        V.rearrange("p (s h d) -> p s h d", h=H, d=65)
                        [:, 4 * c_ + r_, hv_ * 8:(hv_ + 1) * 8, 0:64],
                        bank_ap(r_).rearrange("p (h d) -> p h d", d=64)),
                        waits=[pm])
                    pfree[bkey(r)] = me
                    shared["vevac"] = me
            shared["xtmlast"] = xst[3]

        def emit_bias(c, which, q_src, jw=()):
            if which == 1:
                specs = [(SM_QAP, SM_QA), (SM_KAP, SM_KA)]
            else:
                specs = [(SM_TAP, SM_T0)]
            upd = []
            prev_pm = None
            for (srcp, dst) in specs:
                wz0 = ([prev_pm] if prev_pm else []) + list(jw) + wb(6)
                pm = P.do("tensor", lambda e, s_=srcp: e.transpose(
                    G0[0:1, 0:128], s_, IDENTF[:, :]), waits=wz0)
                prev_pm = pm
                shared[("accrd", which)] = pm
                mr = P.do("vector", lambda e: e.tensor_reduce(
                    SM_RED2, G0[0:1, 0:128], axis=AX.X, op=ALU.max), waits=[pm])
                pfree[6] = mr
                upd.append(P.do("vector", lambda e, d_=dst: e.tensor_copy(
                    d_, SM_RED2), waits=[mr]))
            qa = SM_QA if which == 1 else SM_T0
            m1 = P.do("vector", lambda e, q_=qa: e.tensor_tensor(
                SM_CV, q_, SM_KA, op=ALU.mult), waits=upd)
            m2 = P.do("vector", lambda e: e.tensor_scalar(
                SM_CV, SM_CV, -8.0, -30.0, op0=ALU.mult, op1=ALU.max),
                waits=[m1])
            pm = P.do("tensor", lambda e: e.matmul(
                G0[:, 0:1], ONESF[0:1, 0:128], SM_CV, start=True, stop=True,
                skip_group_check=True), waits=[m2] + wb(6))
            mb = P.do("vector", lambda e: e.tensor_copy(SM_EB, G0[:, 0:1]),
                      waits=[pm] + ([attn_state["explast"]]
                                    if "explast" in attn_state else []))
            pfree[6] = mb
            shared["eblast"] = mb

        attn_state = {"rg": 0, "epi": None}
        pslot_free = {}

        def emit_attn(c, which, q_src):
            nch = 4 * (c + 1)
            NR = 3
            LOOK = 2
            inj = set(range(2, nch - 5, 4))
            perm = list(range(4 * c, nch)) + list(range(0, 4 * c))
            for hp in range(8):
                rg0 = attn_state["rg"]
                exp_by_region = {}
                ctx_first = {0: True, 1: True}
                lastctx = {0: None, 1: None}
                expm = {}
                maskm = {}

                def emit_scores(k, ebr=exp_by_region, expm_=expm, maskm_=maskm,
                                rg0_=rg0, hp_=hp):
                    j = perm[k]
                    rg = (rg0_ + k) % NR
                    mdiag = j - 4 * c
                    q0 = max(mdiag, 0) * 128
                    mm = None
                    for half in range(2):
                        wz = list(wb(2 * rg + half))
                        if k >= NR and half == 0:
                            wz.append(ebr[rg])
                        mm = P.do("tensor", lambda e, j_=j, h_=half, hq_=hp_,
                                  q0_=q0, rg_=rg: e.matmul(
                                      PS8[:, rg_ * 1024 + h_ * 512 + q0_:
                                          rg_ * 1024 + (h_ + 1) * 512],KT[64 * h_:64 * (h_ + 1),
                                         hq_ * S + j_ * 128: hq_ * S + (j_ + 1) * 128],
                                      q_src[64 * h_:64 * (h_ + 1),
                                            hq_ * 512 + q0_: (hq_ + 1) * 512],
                                      start=True, stop=True,
                                      tile_position=(64 * h_, 0),
                                      skip_group_check=True), waits=wz)
                    attn_state["qlast"] = mm
                    sl = k % 4
                    wz = [mm]
                    if sl in pslot_free:
                        wz.append(pslot_free[sl])
                    if k == 0 and hp_ == 0 and "eblast" in shared:
                        wz.append(shared["eblast"])
                    if mdiag >= 2:
                        for half in range(2):
                            me_ = P.do("scalar", lambda e, sl_=sl, rg_=rg,
                                       h_=half, q0_=q0: e.activation(
                                           PBUF[:, sl_ * 1024 + h_ * 512 + q0_:
                                                sl_ * 1024 + (h_ + 1) * 512],
                                           PS8[:, rg_ * 1024 + h_ * 512 + q0_:
                                               rg_ * 1024 + (h_ + 1) * 512],
                                           AF.Exp, bias=SM_EB, scale=0.125),
                                       waits=wz)
                    else:
                        me_ = P.do("scalar", lambda e, sl_=sl, rg_=rg: e.activation(
                            PBUF[:, sl_ * 1024:(sl_ + 1) * 1024],
                            PS8[:, rg_ * 1024:(rg_ + 1) * 1024], AF.Exp,
                            bias=SM_EB, scale=0.125), waits=wz)
                    ebr[rg] = me_
                    expm_[k] = me_
                    attn_state["explast"] = me_
                    if mdiag >= 0:
                        for half in range(2):
                            off = sl * 1024 + half * 512 + mdiag * 128
                            maskm_[(k, half)] = P.do(
                                "vector", lambda e, o_=off: e.tensor_tensor(
                                    PBUF[:, o_: o_ + 128],
                                    PBUF[:, o_: o_ + 128],
                                    TRI[:, :], op=ALU.mult), waits=[me_])

                def emit_ctx(k, expm_=expm, maskm_=maskm, ctxf=ctx_first,
                             lc=lastctx, hp_=hp):
                    j = perm[k]
                    mdiag = j - 4 * c
                    q0 = max(mdiag, 0) * 128
                    sl = k % 4
                    for half in range(2):
                        head = 2 * hp_ + half
                        wz = [maskm_.get((k, half), expm_[k])]
                        if ctxf[half]:
                            wz += wb(6 + half)
                            ctxf[half] = False
                            st_flag = True
                        else:
                            st_flag = False
                        lc[half] = P.do(
                            "tensor", lambda e, j_=j, h_=half, hd_=head, sl_=sl,
                            q0_=q0, st_=st_flag: e.matmul(
                                [CTE, CTO][h_][0:65, q0_:512],
                                V[:, j_ * 1040 + hd_ * 65:
                                   j_ * 1040 + hd_ * 65 + 65],
                                PBUF[:, sl_ * 1024 + h_ * 512 + q0_:
                                     sl_ * 1024 + (h_ + 1) * 512],
                                start=st_, stop=False,
                                skip_group_check=True), waits=wz)
                    pslot_free[sl] = lc[1]

                for k in range(nch):
                    emit_scores(k)
                    if k == 1 and attn_state["epi"] is not None:
                        attn_state["epi"]()
                        attn_state["epi"] = None
                    if k >= LOOK:
                        emit_ctx(k - LOOK)
                    if k in inj:
                        wq_flush(att=((rg0 + k) % NR, expm[k]))
                        wq_quant()
                for k in range(max(nch - LOOK, 0), nch):
                    emit_ctx(k)

                rg_last = (rg0 + nch - 1) % NR
                for rr in range(NR):
                    if rr != rg_last:
                        pfree[2 * rr] = exp_by_region[rr]
                        pfree[2 * rr + 1] = exp_by_region[rr]
                attn_state["rg"] = (rg_last + 1) % NR
                if which == 1:
                    dst = CTXW.rearrange("p (b k x) -> p b k x", b=2, x=514)[
                        :, c % 2, hp, 1:513]
                else:
                    dst = CTX2[:, hp * 512:(hp + 1) * 512]

                def make_epi(rgl=rg_last, lc0=lastctx[0], lc1=lastctx[1],
                             expl=expm[nch - 1], d_=dst):
                    def epi():
                        ml_ = P.do("scalar", lambda e: e.activation(
                            LNSTG, PS8[64:65, 3072:4096], AF.Ln,
                            bias=SM_ZERO[0:1, :]),
                            waits=[lc0, lc1, expl]
                            + attn_state.get("evprev", [])
                            + ([shared["sgnlast"]]
                               if "sgnlast" in shared else []))
                        mr = P.do("scalar", lambda e: lp(
                            lambda: e.activation(
                                BCS[0:1, 0:1024],
                                LNSTG, AF.Exp, scale=-1.0)), waits=[ml_])
                        mrc = [mr, mr]
                        bc = []
                        for half in range(2):
                            pm = P.do("tensor", lambda e, h_=half, rg_=rgl:
                                      e.matmul(
                                          PS8[0:64, rg_ * 1024 + h_ * 512:
                                              rg_ * 1024 + h_ * 512 + 512],
                                          ONESB[0:1, 0:64],
                                          BCS[0:1, h_ * 512: h_ * 512 + 512],
                                          start=True, stop=True,
                                          skip_group_check=True),
                                      waits=[mrc[half]])
                            bc.append(pm)
                        mbc = P.do("scalar", lambda e, rg_=rgl: e.activation(
                            BCS[0:64, 0:1024],
                            PS8[0:64, rg_ * 1024:(rg_ + 1) * 1024],
                            AF.Identity, bias=SM_ZERO[0:64, :]), waits=bc)
                        pfree[2 * rgl] = mbc
                        pfree[2 * rgl + 1] = mbc
                        mev = P.do("vector", lambda e, dd_=d_: e.tensor_tensor(
                            dd_[0:64, :], CTE[0:64, 0:512], BCS[0:64, 0:512],
                            op=ALU.mult), waits=[mbc])
                        mno = P.do("vector", lambda e: e.tensor_tensor(
                            TMPO[0:64, :], CTO[0:64, 0:512],
                            BCS[0:64, 512:1024], op=ALU.mult), waits=[mbc])
                        pfree[7] = mno
                        psh = P.do("tensor", lambda e: e.matmul(
                            PS8[64:128, 3072:3584],
                            IDENT[0:64, 0:64], TMPO[0:64, :],
                            start=True, stop=True, tile_position=(0, 64),
                            skip_group_check=True), waits=[mev, mno])
                        mod = P.do("vector", lambda e, dd_=d_:
                                   e.tensor_copy(
                                       dd_[64:128, :],
                                       PS8[64:128, 3072:3584]),
                                   waits=[psh])
                        pfree[6] = mod
                        attn_state["evprev"] = [mev, mno]
                        attn_state["ctxlast"] = mod
                    return epi

                attn_state["epi"] = make_epi()
            attn_state["epi"]()
            attn_state["epi"] = None

        def emit_halo(c):
            cb = c % 2
            hw = [attn_state["ctxlast"]] + (
                [shared["convlast"]] if "convlast" in shared else [])
            hm = []
            if c == 0:
                hm.append(P.do("gpsimd", lambda e: e.memset(
                    CTXW.rearrange("p (b k x) -> p b k x", b=2, x=514)
                    [:, 0, :, 0:1], 0.0), waits=hw))
            else:
                hm.append(P.do("gpsimd", lambda e, cb_=cb: e.tensor_copy(
                    CTXW.rearrange("p (b k x) -> p b k x", b=2, x=514)[:, cb_, :, 0:1],
                    CTXW.rearrange("p (b k x) -> p b k x", b=2, x=514)
                    [:, 1 - cb_, :, 512:513]), waits=hw))
                hm.append(P.do("gpsimd", lambda e, cb_=cb: e.tensor_copy(
                    CTXW.rearrange("p (b k x) -> p b k x", b=2, x=514)
                    [:, 1 - cb_, :, 513:514],
                    CTXW.rearrange("p (b k x) -> p b k x", b=2, x=514)
                    [:, cb_, :, 1:2]), waits=hw))
            return hm

        def emit_conv(t, hm=(), serial=False):
            tb = t % 2
            q1g = [attn_state["qlast"]] if "qlast" in attn_state else []
            for o in range(8):
                bi = o % 4
                slots = []
                prevw = [MK["wconvT"]]
                for t_ in range(3):
                    sl, mw = lhsp_load(
                        sc_wconvT.rearrange("p (i o) -> p i o", o=D)
                        [:, t_ * 8:(t_ + 1) * 8, o * 128:(o + 1) * 128],
                        waits=prevw)
                    if serial:
                        prevw = [mw]
                    slots.append((sl, mw))
                if serial and o == 0 and t == n_tiles - 1:
                    tap("cvw0", LHSP[:, 0:4096], waits=[m for (_s, m) in slots])
                pm = None
                n_mm = 0
                for t_ in range(3):
                    sl, mw = slots[t_]
                    for i in range(8):
                        wz = [mw]
                        if n_mm == 0:
                            wz += wb(bi) + list(hm)
                        off = tb * 4112 + i * 514 + t_
                        pm = P.do("tensor", lambda e, i_=i, sl_=sl, bi_=bi,
                                  off_=off, st0=(n_mm == 0), sp0=(n_mm == 23):
                                  e.matmul(
                                      bank_ap(bi_),
                                      LHSP[:, sl_ * 1024 + i_ * 128:
                                           sl_ * 1024 + i_ * 128 + 128],
                                      CTXW[:, off_: off_ + 512],
                                      start=st0, stop=sp0,
                                      skip_group_check=True), waits=wz)
                        n_mm += 1
                    lh["free"][sl] = pm
                wz_me = [pm]
                if serial and o == 0 and t == n_tiles - 1:
                    mdbg = P.do("vector", lambda e, bi_=bi: e.tensor_copy(
                        XQO.bitcast(F32)[:, 0:512], bank_ap(bi_)), waits=[pm])
                    tap("cvps0", XQO.bitcast(F32)[:, 0:512], waits=[mdbg])
                    wz_me.append(mdbg)
                me = P.do("scalar", lambda e, bi_=bi, o_=o: e.activation(
                    Q1[:, o_ * 512:(o_ + 1) * 512], bank_ap(bi_), AF.Identity,
                    bias=SM_BCONV[:, o_:o_ + 1]), waits=wz_me + q1g)
                pfree[bkey(bi)] = me
                shared["convevac"] = me
                shared["convlast"] = pm
                mar = P.do("vector", lambda e, o_=o: e.tensor_reduce(
                    SM_RED, Q1[:, o_ * 512:(o_ + 1) * 512], axis=AX.X,
                    op=ALU.max, apply_absolute_value=True),
                    waits=[me] + ([shared[("accrd", 2)]]
                                  if o == 0 and ("accrd", 2) in shared else []))
                if o == 0:
                    mac = P.do("vector", lambda e: e.tensor_copy(
                        SM_TAP, SM_RED), waits=[mar])
                else:
                    mac = P.do("vector", lambda e: e.tensor_tensor(
                        SM_TAP, SM_TAP, SM_RED, op=ALU.max), waits=[mar])
                shared["taccl"] = mac
                if serial:
                    P.close_stage()

        def emit_outproj_ln_quant(t):
            xload = {}
            st_ = shared
            for r in range(2):
                xload[r] = P.do("sync", lambda e, r_=r: e.dma_start(
                    XF.rearrange("p (b c) -> p b c", c=1024)[:, r_ % 2],
                    x_h.rearrange("(s p) c -> s p c", p=128)[4 * t + r_]),
                    waits=(MK3 if t == 0 else [])
                    + ([st_[("xfg", r % 2)]] if ("xfg", r % 2) in st_ else []),
                    dma="xf" + str(r % 2))
            mw_b = P.do("gpsimd", lambda e: e.dma_start(
                BCS[0:1, 0:1024], bout_h.rearrange("(a c) -> a c", c=1024)),
                waits=[attn_state["ctxlast"]], dma="vb")
            strip_done = {}
            for half in range(2):
                first = [True] * 4
                for i in range(8):
                    slw, mww = rhsp_load(
                        sc_woutT.rearrange("p (i o) -> p i o", o=D)
                        [:, i, half * 512:(half + 1) * 512],
                        waits=[MK["woutT"]])
                    for r in range(4):
                        bi = half * 4 + r
                        wz = [mww]
                        if first[r]:
                            wz += wb(bi) + [attn_state["ctxlast"]]
                            first[r] = False
                        pm = P.do("tensor", lambda e, i_=i, sl_=slw, r_=r, bi_=bi:
                                  e.matmul(
                                      bank_ap(bi_),
                                      CTX2[:, i_ * 512 + r_ * 128:
                                           i_ * 512 + r_ * 128 + 128],
                                      RHSP.rearrange("p (s c) -> p s c", c=512)[:, sl_],
                                      start=(i_ == 0), stop=False,
                                      skip_group_check=True), waits=wz)
                        rh["free"][slw] = pm
                for r in range(4):
                    bi = half * 4 + r
                    pm = P.do("tensor", lambda e, h_=half, bi_=bi: e.matmul(
                        bank_ap(bi_), ONESB[0:1, 0:128],
                        BCS[0:1, h_ * 512:(h_ + 1) * 512],
                        start=False, stop=True, skip_group_check=True),
                        waits=[mw_b])
                    strip_done[(r, half)] = pm
                    shared["bcslast"] = pm
            for r in range(4):
                b = r % 2
                madd = None
                for half in range(2):
                    bi = half * 4 + r
                    madd = P.do("vector", lambda e, h_=half, b_=b, bi_=bi:
                                e.tensor_tensor(
                                    XF.rearrange("p (b c) -> p b c", c=1024)
                                    [:, b_, h_ * 512:(h_ + 1) * 512],
                                    bank_ap(bi_),
                                    XF.rearrange("p (b c) -> p b c", c=1024)
                                    [:, b_, h_ * 512:(h_ + 1) * 512], op=ALU.add),
                                waits=[strip_done[(r, half)], xload[r]])
                    pfree[bkey(bi)] = madd
                y1 = XF.rearrange("p (b c) -> p b c", c=1024)[:, b]
                ms = P.do("vector", lambda e, y_=y1: e.reduce_sum(
                    SM_R0, y_, axis=AX.X), waits=[madd])
                mq = P.do("vector", lambda e, y_=y1: e.scalar_tensor_tensor(
                    XQO[:, 0:1024], y_, 0.0, y_, op0=ALU.bypass, op1=ALU.mult,
                    accum_out=SM_R1),
                    waits=[madd] + ([st_[("xqdma", r - 1)]] if ("xqdma", r - 1) in st_ else []))
                mm_ = P.do("vector", lambda e: e.tensor_scalar(
                    SM_R2, SM_R0, 1.0 / 1024.0, None, op0=ALU.mult), waits=[ms])
                mv1 = P.do("vector", lambda e: e.tensor_scalar(
                    SM_R3, SM_R1, 1.0 / 1024.0, None, op0=ALU.mult), waits=[mq])
                mv2 = P.do("vector", lambda e: e.tensor_tensor(
                    SM_R4, SM_R2, SM_R2, op=ALU.mult), waits=[mm_])
                mv3 = P.do("vector", lambda e: e.tensor_tensor(
                    SM_R3, SM_R3, SM_R4, op=ALU.subtract), waits=[mv1, mv2])
                ml1 = P.do("scalar", lambda e: e.activation(
                    SM_R4, SM_R3, AF.Ln, bias=SM_EPS), waits=[mv3])
                ml2 = P.do("scalar", lambda e: e.activation(
                    SM_R3, SM_R4, AF.Exp, scale=-0.5), waits=[ml1])
                mn = P.do("vector", lambda e, y_=y1: e.tensor_scalar(
                    y_, y_, SM_R2, SM_R3, op0=ALU.subtract, op1=ALU.mult),
                    waits=[ml2])
                mst = P.do("scalar", lambda e, r_=r, y_=y1: e.dma_start(
                    sc_y1.rearrange("(s p) c -> s p c", p=128)[4 * t + r_], y_),
                    waits=[mn], dma="dy")
                shared["y1_last"] = mst
                msg1 = P.do("vector", lambda e, y_=y1: e.tensor_scalar(
                    XQT1[:, :], y_, 0.0, -2.0, op0=ALU.is_lt, op1=ALU.mult),
                    waits=[mn])
                msg2 = P.do("vector", lambda e: e.tensor_scalar(
                    XQT1[:, :], XQT1[:, :], 1.0, None, op0=ALU.add), waits=[msg1])
                mb1 = P.do("vector", lambda e, y_=y1: e.tensor_reduce(
                    SM_XB, y_.rearrange("p (a b) -> p a b", b=16), axis=AX.X,
                    op=ALU.max, apply_absolute_value=True),
                    waits=[mn] + ([st_["xrcp"]] if "xrcp" in st_ else []))
                mb2 = P.do("vector", lambda e: e.tensor_scalar(
                    SM_XS, SM_XB, C6, 1e-12, op0=ALU.mult, op1=ALU.max), waits=[mb1])
                mb3a = P.do("scalar", lambda e: e.activation(
                    SM_XB, SM_XS, AF.Ln, bias=SM_ZERO), waits=[mb2])
                mb3 = P.do("scalar", lambda e: e.activation(
                    SM_XR, SM_XB, AF.Exp, scale=-1.0), waits=[mb3a])
                st_["xrcp"] = mb3
                mu1 = P.do("vector", lambda e, y_=y1: e.tensor_scalar(
                    y_.bitcast(I32), y_.bitcast(I32), 0x7FFFFFFF, None,
                    op0=ALU.bitwise_and), waits=[msg1, mst, mb1])
                mu2 = P.do("vector", lambda e, y_=y1: e.tensor_tensor(
                    y_.rearrange("p (a b) -> p a b", b=16),
                    y_.rearrange("p (a b) -> p a b", b=16),
                    SM_XR.unsqueeze(2).broadcast_to([128, 64, 16]), op=ALU.mult),
                    waits=[mu1, mb3])
                mf1 = P.do("vector", lambda e, y_=y1: e.tensor_scalar(
                    XQT2[:, :], y_, 64.0, None, op0=ALU.add), waits=[mu2])
                ma1 = P.do("vector", lambda e: e.tensor_scalar(
                    XQT2[:, :], XQT2[:, :], 64.0, 2.0, op0=ALU.subtract,
                    op1=ALU.min), waits=[mf1])
                mf2 = P.do("vector", lambda e, y_=y1: e.tensor_scalar(
                    XQO[:, :], y_, 128.0, None, op0=ALU.add), waits=[ma1])
                mb1_ = P.do("vector", lambda e: e.tensor_scalar(
                    XQO[:, :], XQO[:, :], 130.0, 0.0, op0=ALU.subtract,
                    op1=ALU.max), waits=[mf2])
                mg0 = P.do("vector", lambda e: e.scalar_tensor_tensor(
                    XQO[:, :], XQO[:, :], 2.0, XQT2[:, :], op0=ALU.min,
                    op1=ALU.add), waits=[mb1_])
                mc1 = P.do("vector", lambda e, y_=y1: e.tensor_scalar(
                    XQT2[:, :], y_, 5.0, 2.0, op0=ALU.is_gt, op1=ALU.mult),
                    waits=[mg0])
                st_[("xfg", r % 2)] = mc1
                if r + 2 < 4:
                    xload[r + 2] = P.do("sync", lambda e, r_=r + 2: e.dma_start(
                        XF.rearrange("p (b c) -> p b c", c=1024)[:, r_ % 2],
                        x_h.rearrange("(s p) c -> s p c", p=128)[4 * t + r_]),
                        waits=[mc1], dma="xf" + str(r % 2))
                mg2 = P.do("vector", lambda e: e.tensor_tensor(
                    XQO[:, :], XQO[:, :], XQT2[:, :], op=ALU.add), waits=[mc1])
                mt1 = P.do("vector", lambda e: e.tensor_tensor(
                    XQO[:, :], XQO[:, :], XQT1[:, :], op=ALU.mult),
                    waits=[mg2, msg2])
                st_["sgnlast"] = mt1
                mqf = P.do("vector", lambda e: e.tensor_tensor(
                    XQO.rearrange("p (a b) -> p a b", b=16),
                    XQO.rearrange("p (a b) -> p a b", b=16),
                    SM_XS.unsqueeze(2).broadcast_to([128, 64, 16]), op=ALU.mult),
                    waits=[mt1])
                st_[("xqdma", r)] = P.do("scalar", lambda e, r_=r: e.dma_start(
                    sc_xq.rearrange("(s p) c -> s p c", p=128)[4 * t + r_],
                    XQO[:, :]), waits=[mqf], dma="dxq")

        for c in range(n_tiles):
            emit_xt_qkv(c)
            if c == 0:
                tap("wsc", sc_wqkvT)
                tap("wcc", sc_wconvT)
                tap("xt0", XT[:, :])
                tap("q10", Q1[:, :])
                tap("kt0", KT[:, :])
                tap("v0", V[:, :])
            emit_bias(c, 1, Q1,
                      jw=[shared["qaccl"], shared["vevac"]])
            if c == 0:
                tap("eb0", SM[:, 0:64])
            emit_attn(c, 1, Q1)
            if c == 0:
                tap("ctx0", CTXW[:, :])
            hm_halo = emit_halo(c)
            if c >= 1:
                emit_conv(c - 1, hm=hm_halo)
                if c == 1:
                    tap("q20", Q1[:, :])
                emit_bias(c - 1, 2, Q1, jw=[shared["taccl"]])
                emit_attn(c - 1, 2, Q1)
                if c == 1:
                    tap("ctx20", CTX2[:, :])
                emit_outproj_ln_quant(c - 1)
                if c == 1:
                    tap("xq0", XQO[:, :])
        tl = n_tiles - 1
        mts = P.do("gpsimd", lambda e: e.memset(
            CTXW.rearrange("p (b k x) -> p b k x", b=2, x=514)
            [:, tl % 2, :, 513:514], 0.0),
            waits=[attn_state["ctxlast"]]
            + ([shared["convlast"]] if "convlast" in shared else []))
        emit_conv(tl, hm=[mts], serial=DEV_SERIAL_CONV)
        tap("q2t", Q1[:, :])
        emit_bias(tl, 2, Q1, jw=[shared["taccl"]])
        tap("eb2", SM[:, 0:64])
        emit_attn(tl, 2, Q1)
        tap("ctx2t", CTX2[:, :])
        emit_outproj_ln_quant(tl)
        tap("xqsc", sc_xq, waits=list(P.fence) + [shared[("xqdma", 3)]])
        tap("y1sc", sc_y1, waits=list(P.fence) + [shared["y1_last"]])
        while wqs["i"] < len(wq_pieces):
            wq_flush()
            wq_quant()
        wq_flush()
        P.close_stage()

        if do_pc:
            pbf = list(P.fence)
            mwg = P.do("sync", lambda e: e.dma_start(WG[:, :], sc_wgT),
                       waits=pbf + [wqs["tf"]], dma="wgt")
            mwu = P.do("sync", lambda e: e.dma_start(WU[:, :], sc_wuT),
                       waits=pbf + [wqs["tf"]], dma="wgu")
            mwd = P.do("sync", lambda e: e.dma_start(WD[:, :], sc_wdT),
                       waits=pbf + [wqs["tf"]], dma="wgd")
            pcg = {}

            def emit_xqst(c):
                B45 = [PS8b[:, 4096:5120], PS8b[:, 5120:6144]]
                mtr = []
                for r in range(4):
                    b = r % 2
                    gk = ("xqs", c * 4 + r - 2)
                    md = P.do("sync", lambda e, r_=r, b_=b, c_=c: e.dma_start(
                        XQS.rearrange("p (b c) -> p b c", c=1024)[:, b_],
                        sc_xq.rearrange("(s p) c -> s p c", p=128)[4 * c_ + r_]),
                        waits=pbf + [shared[("xqdma", 3)]]
                        + ([pcg[gk]] if gk in pcg else []), dma="xs" + str(b))
                    pm = None
                    for i in range(8):
                        wz = [md] + (wb(4 + r % 2) if i == 0 else [])
                        pm = P.do("tensor", lambda e, i_=i, b_=b, r_=r,
                                  a_=B45[r % 2]: e.transpose(
                                      a_.rearrange("p (i c) -> p i c", c=128)[:, i_],
                                      XQS.rearrange("p (b c) -> p b c", c=1024)[:, b_]
                                      .rearrange("p (i c) -> p i c", c=128)[:, i_],
                                      IDENT[:, :]), waits=wz)
                    pcg[("xqs", c * 4 + r)] = pm
                    me = P.do("vector", lambda e, r_=r, a_=B45[r % 2]: e.tensor_copy(
                        XQTT.rearrange("p (i c) -> p i c", c=512)
                        [:, :, r_ * 128:(r_ + 1) * 128],
                        a_.rearrange("p (i c) -> p i c", c=128)),
                        waits=[pm])
                    pfree[bkey(4 + r % 2)] = me
                    mtr.append(me)
                pcg[("mtr", c)] = mtr

            def emit_A1(g):
                c, r = divmod(g, 4)
                b = r % 2
                SGp = [SG, SG2][g % 2]
                mtr = pcg[("mtr", c)]
                gky = ("y1s", g - 2)
                my1 = P.do("gpsimd", lambda e, r_=r, b_=b, c_=c: e.dma_start(
                    Y1S.rearrange("p (b c) -> p b c", c=1024)[:, b_],
                    sc_y1.rearrange("(s p) c -> s p c", p=128)[4 * c_ + r_]),
                    waits=pbf + [shared["y1_last"]]
                    + ([pcg[gky]] if gky in pcg else []), dma="ys" + str(b))
                hfw = pcg.get(("hflast", g - 1), [])
                hm = []
                for cc in range(6):
                    L = 512 if cc < 5 else 256
                    pg = None
                    pu = None
                    for i in range(8):
                        wz = [mwg, mtr[r]] + (wb(cc % 2) if i == 0 else [])
                        pg = P.do("tensor", lambda e, i_=i, cc_=cc, r_=r, L_=L,
                                  bk_=cc % 2:
                                  e.matmul(
                                      bank_ap(bk_)[:, 0:L_],
                                      XQTT[:, i_ * 512 + r_ * 128:
                                           i_ * 512 + r_ * 128 + 128],
                                      WG[:, i_ * DFF + cc_ * 512:
                                         i_ * DFF + cc_ * 512 + L_],
                                      start=(i_ == 0), stop=(i_ == 7),
                                      skip_group_check=True), waits=wz)
                    msl = P.do("scalar", lambda e, cc_=cc, L_=L, bk_=cc % 2,
                               s_=SGp: e.activation(
                                   s_[:, cc_ * 512: cc_ * 512 + L_],
                                   bank_ap(bk_)[:, 0:L_], AF.Silu),
                               waits=[pg]
                               + ([pcg[("sgu", g - 2)]]
                                  if ("sgu", g - 2) in pcg else []))
                    pfree[cc % 2] = msl
                    for i in range(8):
                        wz = [mwu, mtr[r]] + (wb(2 + cc % 2) if i == 0 else [])
                        pu = P.do("tensor", lambda e, i_=i, cc_=cc, r_=r, L_=L,
                                  bk_=2 + cc % 2:
                                  e.matmul(
                                      bank_ap(bk_)[:, 0:L_],
                                      XQTT[:, i_ * 512 + r_ * 128:
                                           i_ * 512 + r_ * 128 + 128],
                                      WU[:, i_ * DFF + cc_ * 512:
                                         i_ * DFF + cc_ * 512 + L_],
                                      start=(i_ == 0), stop=(i_ == 7),
                                      skip_group_check=True), waits=wz)
                    mue = P.do("scalar", lambda e, cc_=cc, L_=L, bk_=2 + cc % 2:
                               e.activation(
                                   HF[:, cc_ * 512: cc_ * 512 + L_],
                                   bank_ap(bk_)[:, 0:L_], AF.Identity,
                                   bias=SM_ZERO), waits=[pu] + hfw)
                    pfree[2 + cc % 2] = mue
                    mh = P.do("vector", lambda e, cc_=cc, L_=L, s_=SGp:
                              e.tensor_tensor(
                                  HF[:, cc_ * 512: cc_ * 512 + L_],
                                  HF[:, cc_ * 512: cc_ * 512 + L_],
                                  s_[:, cc_ * 512: cc_ * 512 + L_], op=ALU.mult),
                              waits=[mue, msl])
                    hm.append(mh)
                    mbr = P.do("vector", lambda e, cc_=cc, L_=L: e.tensor_reduce(
                        S2B[:, cc_ * 32: cc_ * 32 + L_ // 16],
                        HF[:, cc_ * 512: cc_ * 512 + L_]
                        .rearrange("p (a b) -> p a b", b=16),
                        axis=AX.X, op=ALU.max, apply_absolute_value=True),
                        waits=[mh])
                pcg[("hbr", g)] = mbr
                pcg[("hm", g)] = hm
                pcg[("my1", g)] = my1

            def emit_chain(g):
                SGp = [SG, SG2][g % 2]
                HQp = [HQ, HQ2][g % 2]
                hm = pcg.pop(("hm", g))
                msg1 = P.do("vector", lambda e, s_=SGp: e.tensor_scalar(
                    s_[:, :], HF[:, :], 0.0, -2.0, op0=ALU.is_lt, op1=ALU.mult),
                    waits=hm)
                msg2 = P.do("vector", lambda e, s_=SGp: e.tensor_scalar(
                    s_[:, :], s_[:, :], 1.0, None, op0=ALU.add), waits=[msg1])
                mb1 = pcg.pop(("hbr", g))
                mb2 = P.do("vector", lambda e: e.tensor_scalar(
                    S2_HS, S2B, C6, 1e-12, op0=ALU.mult, op1=ALU.max),
                    waits=[mb1])
                mb3a = P.do("scalar", lambda e: e.activation(
                    S2_HB, S2_HS, AF.Ln, bias=SM_ZERO), waits=[mb2])
                mb3 = P.do("scalar", lambda e: e.activation(
                    S2_HR, S2_HB, AF.Exp, scale=-1.0), waits=[mb3a])
                pcg["hrcp"] = mb3
                mu1 = P.do("vector", lambda e: e.tensor_scalar(
                    HF.bitcast(I32)[:, :], HF.bitcast(I32)[:, :], 0x7FFFFFFF,
                    None, op0=ALU.bitwise_and), waits=[msg1, mb1])
                mu2 = P.do("vector", lambda e: e.tensor_tensor(
                    HF.rearrange("p (a b) -> p a b", b=16),
                    HF.rearrange("p (a b) -> p a b", b=16),
                    S2_HR.unsqueeze(2).broadcast_to([128, 176, 16]),
                    op=ALU.mult), waits=[mu1, mb3])
                mf1 = P.do("vector", lambda e: e.tensor_scalar(
                    HQT2[:, :], HF[:, :], 64.0, None, op0=ALU.add),
                    waits=[mu2] + ([pcg[("hqt2", g - 1)]]
                                   if ("hqt2", g - 1) in pcg else []))
                ma1 = P.do("vector", lambda e: e.tensor_scalar(
                    HQT2[:, :], HQT2[:, :], 64.0, 2.0,
                    op0=ALU.subtract, op1=ALU.min), waits=[mf1])
                mf2 = P.do("vector", lambda e, h_=HQp: e.tensor_scalar(
                    h_[:, :], HF[:, :], 128.0, None, op0=ALU.add),
                    waits=[ma1] + pcg.get(("hqpe", g - 2), []))
                mb1_ = P.do("vector", lambda e, h_=HQp: e.tensor_scalar(
                    h_[:, :], h_[:, :], 130.0, 0.0, op0=ALU.subtract,
                    op1=ALU.max), waits=[mf2])
                mg0a = P.do("vector", lambda e, h_=HQp: e.tensor_scalar(
                    h_[:, :], h_[:, :], 2.0, None, op0=ALU.min), waits=[mb1_])
                mg0 = P.do("vector", lambda e, h_=HQp: e.tensor_tensor(
                    h_[:, :], h_[:, :], HQT2[:, :], op=ALU.add), waits=[mg0a])
                mc1 = P.do("vector", lambda e: e.tensor_scalar(
                    HQT2[:, :], HF[:, :], 5.0, 2.0, op0=ALU.is_gt,
                    op1=ALU.mult), waits=[mg0])
                pcg[("hflast", g)] = [mc1]
                mg2 = P.do("vector", lambda e, h_=HQp: e.tensor_tensor(
                    h_[:, :], h_[:, :], HQT2[:, :], op=ALU.add),
                    waits=[mc1])
                pcg[("hqt2", g)] = mg2
                mt1 = P.do("vector", lambda e, s_=SGp, h_=HQp: e.tensor_tensor(
                    h_[:, :], h_[:, :], s_[:, :], op=ALU.mult),
                    waits=[mg2, msg2])
                pcg[("sgu", g)] = mt1
                mqf = P.do("vector", lambda e, h_=HQp: e.tensor_tensor(
                    h_.rearrange("p (a b) -> p a b", b=16),
                    h_.rearrange("p (a b) -> p a b", b=16),
                    S2_HS.unsqueeze(2).broadcast_to([128, 176, 16]),
                    op=ALU.mult), waits=[mt1])
                pcg[("mqf", g)] = mqf

            def emit_Bpe(g):
                HQp = [HQ, HQ2][g % 2]
                mqf = pcg.pop(("mqf", g))
                mtp = []
                lastpm = None
                rounds = [(0, 8, 4), (8, 8, 5), (16, 6, 4)]
                for (b0, nb_, bidx) in rounds:
                    bb_ap = PS8b[:, bidx * 1024:(bidx + 1) * 1024]
                    pm = None
                    for j in range(nb_):
                        wz = [mqf] + (wb(bidx) if j == 0 else [])
                        pm = P.do("tensor", lambda e, b_=b0 + j, j_=j, a_=bb_ap,
                                  h_=HQp:
                                  e.transpose(
                                      a_.rearrange("p (i c) -> p i c", c=128)[:, j_],
                                      h_.rearrange("p (i c) -> p i c", c=128)[:, b_],
                                      IDENT[:, :]), waits=wz)
                    me = P.do("vector", lambda e, b0_=b0, n_=nb_, a_=bb_ap:
                              e.tensor_copy(
                                  HQT[:, b0_ * 128:(b0_ + n_) * 128],
                                  a_[:, 0:n_ * 128]), waits=[pm])
                    pfree[bkey(bidx)] = me
                    mtp.append(me)
                    lastpm = pm
                pcg[("hqpe", g)] = [lastpm]
                pdw = {}
                for half in range(2):
                    pm = None
                    for j in range(22):
                        wz = [mwd, mtp[min(j // 8, 2)]] + (
                            wb(6 + half) if j == 0 else [])
                        pm = P.do("tensor", lambda e, j_=j, h_=half: e.matmul(
                            [G0, G1][h_][:, 0:512],
                            HQT[:, j_ * 128:(j_ + 1) * 128],
                            WD[:, j_ * D + h_ * 512: j_ * D + h_ * 512 + 512],
                            start=(j_ == 0), stop=(j_ == 21),
                            skip_group_check=True), waits=wz)
                    pdw[half] = pm
                pcg[("pdw", g)] = pdw

            def emit_Bfin(g):
                c, r = divmod(g, 4)
                b = r % 2
                pdw = pcg.pop(("pdw", g))
                my1 = pcg.pop(("my1", g))
                madd = None
                gko = ("ost", g - 2)
                for half in range(2):
                    madd = P.do("vector", lambda e, h_=half, b_=b: e.tensor_tensor(
                        OUTF.rearrange("p (b c) -> p b c", c=1024)
                        [:, b_, h_ * 512:(h_ + 1) * 512],
                        [G0, G1][h_][:, 0:512],
                        Y1S.rearrange("p (b c) -> p b c", c=1024)
                        [:, b_, h_ * 512:(h_ + 1) * 512], op=ALU.add),
                        waits=[pdw[half], my1]
                        + ([pcg[gko]] if gko in pcg else []))
                    pfree[bkey(6 + half)] = madd
                pcg[("y1s", g)] = madd
                yo = OUTF.rearrange("p (b c) -> p b c", c=1024)[:, b]
                ms = P.do("vector", lambda e, y_=yo: e.reduce_sum(
                    S2_R0, y_, axis=AX.X), waits=[madd])
                mq = P.do("vector", lambda e, y_=yo: e.scalar_tensor_tensor(
                    HQT[:, 0:1024], y_, 0.0, y_, op0=ALU.bypass, op1=ALU.mult,
                    accum_out=S2_R1), waits=[madd])
                mm_ = P.do("vector", lambda e: e.tensor_scalar(
                    S2_R2, S2_R0, 1.0 / 1024.0, None, op0=ALU.mult), waits=[ms])
                mv1 = P.do("vector", lambda e: e.tensor_scalar(
                    S2_R3, S2_R1, 1.0 / 1024.0, None, op0=ALU.mult), waits=[mq])
                mv2 = P.do("vector", lambda e: e.tensor_tensor(
                    S2_R4, S2_R2, S2_R2, op=ALU.mult), waits=[mm_])
                mv3 = P.do("vector", lambda e: e.tensor_tensor(
                    S2_R3, S2_R3, S2_R4, op=ALU.subtract), waits=[mv1, mv2])
                ml1 = P.do("scalar", lambda e: e.activation(
                    S2_R4, S2_R3, AF.Ln, bias=SM_EPS), waits=[mv3])
                ml2 = P.do("scalar", lambda e: e.activation(
                    S2_R3, S2_R4, AF.Exp, scale=-0.5), waits=[ml1])
                mn = P.do("vector", lambda e, y_=yo: e.tensor_scalar(
                    y_, y_, S2_R2, S2_R3, op0=ALU.subtract, op1=ALU.mult),
                    waits=[ml2])
                pcg[("ost", g)] = P.do(
                    "sync", lambda e, r_=r, y_=yo, c_=c: e.dma_start(
                        out_h.rearrange("(s p) c -> s p c", p=128)[4 * c_ + r_],
                        y_), waits=[mn], dma="out")
                pcg[("hqw", g)] = [mq]

            NG = 4 * n_tiles
            emit_xqst(0)
            for g in range(NG):
                if g >= 2:
                    emit_Bpe(g - 2)
                if g >= 1:
                    emit_chain(g - 1)
                if g >= 2:
                    emit_Bfin(g - 2)
                emit_A1(g)
                if g % 4 == 3 and g + 1 < NG:
                    emit_xqst((g + 1) // 4)
            emit_Bpe(NG - 2)
            emit_chain(NG - 1)
            emit_Bfin(NG - 2)
            emit_Bpe(NG - 1)
            emit_Bfin(NG - 1)
            P.do("sync", lambda e: e.nop(), waits=[("out", P.sems["out"][1])])

        @block.tensor
        def _(eng):
            P.emit("tensor", eng)

        @block.scalar
        def _(eng):
            P.emit("scalar", eng)

        @block.vector
        def _(eng):
            P.emit("vector", eng)

        @block.gpsimd
        def _(eng):
            P.emit("gpsimd", eng)

        @block.sync
        def _(eng):
            P.emit("sync", eng)

    return nc, tap_h


def kernel(**inputs):
    x = np.ascontiguousarray(np.asarray(inputs["L0_attn_x"], dtype=np.float32))
    common = {
        "wqkv": np.ascontiguousarray(inputs["L0_attn_Wqkv_w"], dtype=np.float32),
        "bqkv": np.ascontiguousarray(inputs["L0_attn_Wqkv_b"], dtype=np.float32),
        "wconv": np.ascontiguousarray(inputs["L0_attn_conv_w"], dtype=np.float32),
        "bconv": np.ascontiguousarray(inputs["L0_attn_conv_b"], dtype=np.float32),
        "wout": np.ascontiguousarray(inputs["L0_attn_out_w"], dtype=np.float32),
        "bout": np.ascontiguousarray(inputs["L0_attn_out_b"], dtype=np.float32),
        "wg": np.ascontiguousarray(inputs["L0_ffn_w_gate"], dtype=np.float32),
        "wu": np.ascontiguousarray(inputs["L0_ffn_w_up"], dtype=np.float32),
        "wd": np.ascontiguousarray(inputs["L0_ffn_w_down"], dtype=np.float32),
    }
    nc, _taps = build_program()
    in_maps = [dict(common, x=x[b]) for b in range(N_CORES)]
    res = run_bass_kernel_spmd(nc, in_maps, core_ids=list(range(N_CORES)))
    return np.stack([res.results[b]["out"] for b in range(N_CORES)], axis=0)


if __name__ == "__main__":
    nc, _ = build_program()
    n = {}
    for op in []:
        pass
    print("built OK")


# version 37
# speedup vs baseline: 1.0279x; 1.0279x over previous
import contextlib
import numpy as np
import ml_dtypes
import concourse.bass as bass
import concourse.mybir as mybir
from concourse.bass_utils import run_bass_kernel_spmd

F32 = mybir.dt.float32
BF16 = mybir.dt.bfloat16
I32 = mybir.dt.int32
U8 = mybir.dt.uint8
ALU = mybir.AluOpType
AF = mybir.ActivationFunctionType
AX = mybir.AxisListType

S, D, H, DH = 4096, 1024, 16, 64
NT, TT, NS = 8, 512, 32
DFF = 2816
C6 = float(np.float32(1.0 / 6.0))
N_CORES = 8
SBUF_LIMIT = 212000
DEV_SERIAL_CONV = False
ENGS = ("tensor", "scalar", "vector", "gpsimd", "sync")
ENG2SEM = {"tensor": "pe", "scalar": "act", "vector": "dve", "gpsimd": "gp"}


class Arena:
    def __init__(self, ap_u8, limit):
        self.u8, self.off, self.limit = ap_u8, 0, limit

    def alloc(self, cols, dt, align=64):
        esz = {F32: 4, BF16: 2, I32: 4}[dt]
        self.off = (self.off + align - 1) // align * align
        start = self.off
        self.off += cols * esz
        assert self.off <= self.limit, f"SBUF overflow {self.off} > {self.limit}"
        return self.u8.bitcast(dt)[:, start // esz: start // esz + cols]


class Op:
    __slots__ = ("eng", "fn", "waits", "incs")

    def __init__(self, eng, fn, waits, incs):
        self.eng, self.fn, self.waits, self.incs = eng, fn, waits, incs


class Prog:

    def __init__(self):
        self.ops, self.sems = [], {}
        self.fence = []
        self.fence_seen = set()

    def add_sem(self, name, handle):
        self.sems[name] = [handle, 0]

    def do(self, eng, fn, waits=(), dma=None):
        w = list(waits)
        if eng != "sync" and eng not in self.fence_seen:
            w += self.fence
            self.fence_seen.add(eng)
        if dma is not None:
            semn, by = dma, 16
        else:
            semn, by = ENG2SEM.get(eng), 1
        wl = [(self.sems[n][0], v) for (n, v) in w if v > 0]
        if semn is None:
            self.ops.append(Op(eng, fn, wl, []))
            return None
        s = self.sems[semn]
        s[1] += by
        self.ops.append(Op(eng, fn, wl, [(s[0], by)]))
        return (semn, s[1])

    def close_stage(self):
        self.fence = [(n, self.sems[n][1]) for n in ("pe", "act", "dve", "gp")]
        self.fence_seen = set()

    def emit(self, eng_name, eng):
        wm = {}
        for op in self.ops:
            if op.eng != eng_name:
                continue
            for (h, v) in op.waits:
                k = h.num
                if wm.get(k, -1) >= v:
                    continue
                eng.wait_ge(h, v)
                wm[k] = v
            inst = op.fn(eng)
            for (h, a) in op.incs:
                assert inst is not None
                inst.then_inc(h, a)


def build_program(taps=None, n_tiles=NT, do_pc=True):
    taps = taps or {}
    nc = bass.Bass("TRN2", target_bir_lowering=False)

    x_h = nc.dram_tensor("x", [S, D], F32, kind="ExternalInput").ap()
    wqkv_h = nc.dram_tensor("wqkv", [3 * D, D], F32, kind="ExternalInput").ap()
    bqkv_h = nc.dram_tensor("bqkv", [3 * D], F32, kind="ExternalInput").ap()
    wconv_h = nc.dram_tensor("wconv", [D, D, 3], F32, kind="ExternalInput").ap()
    bconv_h = nc.dram_tensor("bconv", [D], F32, kind="ExternalInput").ap()
    wout_h = nc.dram_tensor("wout", [D, D], F32, kind="ExternalInput").ap()
    bout_h = nc.dram_tensor("bout", [D], F32, kind="ExternalInput").ap()
    wg_h = nc.dram_tensor("wg", [DFF, D], F32, kind="ExternalInput").ap()
    wu_h = nc.dram_tensor("wu", [DFF, D], F32, kind="ExternalInput").ap()
    wd_h = nc.dram_tensor("wd", [D, DFF], F32, kind="ExternalInput").ap()
    out_h = nc.dram_tensor("out", [S, D], F32, kind="ExternalOutput").ap()

    sc_wqkvT = nc.dram_tensor("sc_wqkvT", [128, 8 * 3 * D], BF16).ap()
    sc_wconvT = nc.dram_tensor("sc_wconvT", [128, 24 * D], BF16).ap()
    sc_woutT = nc.dram_tensor("sc_woutT", [128, 8 * D], BF16).ap()
    sc_wgT = nc.dram_tensor("sc_wgT", [128, 8 * DFF], BF16).ap()
    sc_wuT = nc.dram_tensor("sc_wuT", [128, 8 * DFF], BF16).ap()
    sc_wdT = nc.dram_tensor("sc_wdT", [128, 22 * D], BF16).ap()
    sc_xq = nc.dram_tensor("sc_xq", [S, D], BF16).ap()
    sc_y1 = nc.dram_tensor("sc_y1", [S, D], F32).ap()

    tap_h = {n: nc.dram_tensor("tap_" + n, sp[0], sp[1], kind="ExternalOutput").ap()
             for n, sp in taps.items()}

    P = Prog()

    with (
        nc.sbuf_tensor("mega", [128, SBUF_LIMIT], U8) as mega_t,
        nc.psum_tensor("PS8", [128, 4096], F32) as PS8_t,
        contextlib.ExitStack() as _stk,
    ):
        mega = mega_t.ap()
        PS8 = PS8_t.ap()
        PS8b = PS8.bitcast(BF16)
        SP = PS8[:, 0:2048]
        CTE = PS8[:, 3072:3584]
        CTO = PS8[:, 3584:4096]
        G0 = PS8[:, 3072:3584]
        G1 = PS8[:, 3584:4096]
        G0b = PS8b[:, 6144:7168]
        G1b = PS8b[:, 7168:8192]
        CTEb = PS8b[:, 6144:7168]
        SEM_NAMES = ["pe", "act", "dve", "gp",
                     "dwq", "pin0", "pin1", "pin2", "pinc", "pinc2",
                     "flq", "flo", "flc",
                     "lh0", "lh1", "lh2", "lh3", "rh0", "rh1",
                     "xm", "xf0", "xf1", "vb", "dy", "dxq",
                     "wqe", "wqf", "wgt", "wgu", "wgd",
                     "xs0", "xs1", "ys0", "ys1", "out"]
        for nm in SEM_NAMES:
            P.add_sem(nm, _stk.enter_context(nc.semaphore("s_" + nm)))
        block = _stk.enter_context(nc.Block())

        A = Arena(mega, SBUF_LIMIT)
        KT = A.alloc(8 * S, BF16)
        V = A.alloc(NS * 1040, BF16)
        CTXW = A.alloc(2 * 8 * 514, BF16)
        PBUF = A.alloc(2 * 2048, BF16)
        Q1 = A.alloc(8 * TT, BF16)
        XT = A.alloc(8 * TT, BF16)
        CTX2 = XT
        XTM = A.alloc(1024, BF16)
        XF = A.alloc(2 * 1024, F32)
        LHSP = A.alloc(4 * 1024, BF16)
        RHSP = A.alloc(2 * 512, BF16)
        XQO = A.alloc(1024, BF16)
        XQT12 = A.alloc(2 * 1024, BF16)
        XQT1 = XQT12[:, 0:1024]
        XQT2 = XQT12[:, 1024:2048]
        BCS = A.alloc(1024, BF16)
        TMPO = A.alloc(512, BF16)
        ONESB = A.alloc(128, BF16)
        ONESF = A.alloc(128, F32)
        IDENTF = A.alloc(128, F32)
        TRI = A.alloc(128, BF16)
        WQF = A.alloc(256, F32)
        WQU = A.alloc(256, F32)
        WQB = A.alloc(2 * 256, BF16)
        WQO = A.alloc(256, BF16)
        WQT = A.alloc(256, BF16)
        WQS = A.alloc(48, F32)
        pb_dead_top = A.off
        IDENT = A.alloc(128, BF16)
        SM = A.alloc(384, F32)
        pb_top = A.off

        SM_BQKV = SM[:, 0:24]
        SM_BCONV = SM[:, 24:32]
        SM_QA = SM[0:1, 32:33]
        SM_KA = SM[0:1, 33:34]
        SM_T0 = SM[0:1, 34:35]
        SM_CV = SM[0:1, 35:36]
        SM_EB = SM[:, 36:37]
        SM_EPS = SM[:, 37:38]
        SM_C64 = SM[:, 38:39]
        SM_C128 = SM[:, 39:40]
        SM_R0 = SM[:, 40:41]
        SM_R1 = SM[:, 41:42]
        SM_R2 = SM[:, 42:43]
        SM_R3 = SM[:, 43:44]
        SM_R4 = SM[:, 44:45]
        SM_ZERO = SM[:, 46:47]
        SM_QAP = SM[:, 49:50]
        SM_KAP = SM[:, 50:51]
        SM_TAP = SM[:, 51:52]
        LNSTG = XQT12.bitcast(F32)[0:1, 0:1024]
        SM_RED = SM[:, 60:61]
        SM_RED2 = SM[0:1, 62:63]
        SM_XB = SM[:, 64:128]
        SM_XS = SM[:, 128:192]
        SM_XR = SM[:, 192:256]
        WQ_B = WQS[:, 0:16]
        WQ_S = WQS[:, 16:32]
        WQ_R = WQS[:, 32:48]

        A2 = Arena(mega, SBUF_LIMIT)
        WG = A2.alloc(8 * DFF, BF16)
        WU = A2.alloc(8 * DFF, BF16)
        WD = A2.alloc(22 * D, BF16)
        XQS = A2.alloc(2 * 1024, BF16)
        XQTT = A2.alloc(8 * TT, BF16)
        Y1S = A2.alloc(2 * 1024, BF16)
        SG = A2.alloc(DFF, BF16)
        HF = A2.alloc(DFF, F32)
        HQ = A2.alloc(DFF, BF16)
        HQT = A2.alloc(22 * 128, BF16)
        HQT2 = A2.alloc(DFF, BF16)
        SG2 = A2.alloc(DFF, BF16)
        HQ2 = A2.alloc(DFF, BF16)
        OUTF = A2.alloc(2 * 1024, F32)
        SM2 = A2.alloc(640, F32)
        S2B = A2.alloc(176, F32)
        assert A2.off <= pb_dead_top, (A2.off, pb_dead_top)
        S2_R0 = SM2[:, 0:1]
        S2_R1 = SM2[:, 1:2]
        S2_R2 = SM2[:, 2:3]
        S2_R3 = SM2[:, 3:4]
        S2_R4 = SM2[:, 4:5]
        S2_HB = SM2[:, 64:240]
        S2_HS = SM2[:, 240:416]
        S2_HR = SM2[:, 416:592]

        def lp(call):
            with nc.allow_low_precision(reason="bf16 softmax normalizer (validated)"):
                return call()

        def tap(name, src, waits=None):
            if name in tap_h:
                P.do("sync", lambda e, s=src, d=tap_h[name]: e.dma_start(d, s),
                     waits=list(P.fence) if waits is None else waits, dma="out")

        pfree = {}

        def bank_ap(idx):
            return PS8[:, idx * 512:(idx + 1) * 512]

        def bkey(idx):
            return idx

        def wb(idx):
            m = pfree.get(idx)
            return [m] if m else []

        P.do("gpsimd", lambda e: e.memset(ONESB[:, :], 1.0))
        P.do("gpsimd", lambda e: e.memset(ONESF[:, :], 1.0))
        P.do("gpsimd", lambda e: e.memset(IDENT[:, :], 0.0))
        P.do("gpsimd", lambda e: e.affine_select(
            out=IDENT[:, :], in_=IDENT[:, :], compare_op=ALU.not_equal,
            fill=1.0, base=0, pattern=[[-1, 128]], channel_multiplier=1))
        P.do("gpsimd", lambda e: e.memset(IDENTF[:, :], 0.0))
        P.do("gpsimd", lambda e: e.affine_select(
            out=IDENTF[:, :], in_=IDENTF[:, :], compare_op=ALU.not_equal,
            fill=1.0, base=0, pattern=[[-1, 128]], channel_multiplier=1))
        P.do("gpsimd", lambda e: e.memset(TRI[:, :], 1.0))
        P.do("gpsimd", lambda e: e.affine_select(
            out=TRI[:, :], in_=TRI[:, :], compare_op=ALU.is_ge, fill=0.0,
            base=0, pattern=[[1, 128]], channel_multiplier=-1))
        P.do("gpsimd", lambda e: e.memset(
            V.rearrange("p (r h d) -> p r h d", h=H, d=65)[:, :, :, 64:65], 1.0))
        P.do("gpsimd", lambda e: e.memset(SM[0:1, 32:36], 1e-6))
        P.do("gpsimd", lambda e: e.memset(SM[:, 36:64], 0.0))
        P.do("gpsimd", lambda e: e.memset(SM_EPS, 1e-5))
        P.do("gpsimd", lambda e: e.memset(SM_C64, 64.0))
        P.do("gpsimd", lambda e: e.memset(SM_C128, 128.0))
        P.close_stage()

        P.do("sync", lambda e: e.dma_start(
            WQF[0:24, 0:128], bqkv_h.rearrange("(a b) -> a b", b=128)), dma="dwq")
        m = P.do("sync", lambda e: e.dma_start(
            WQF[0:8, 128:256], bconv_h.rearrange("(a b) -> a b", b=128)), dma="dwq")
        m2 = P.do("tensor", lambda e: e.transpose(
            G0[:, 0:24], WQF[0:24, 0:128], IDENTF[0:24, 0:24]), waits=[m])
        mc_ = P.do("vector", lambda e: e.tensor_copy(SM_BQKV, G0[:, 0:24]), waits=[m2])
        m2 = P.do("tensor", lambda e: e.transpose(
            G0[:, 24:32], WQF[0:8, 128:256], IDENTF[0:8, 0:8]), waits=[m, mc_])
        P.do("vector", lambda e: e.tensor_copy(SM_BCONV, G0[:, 24:32]), waits=[m2])
        P.close_stage()

        PIN = [HF.bitcast(BF16)[:, 0:1024], HF.bitcast(BF16)[:, 1024:2048],
               KT[:, 0:1024]]
        PINC = [XQTT[:, 0:3072], KT[:, 1024:4096]]
        PSTG = [OUTF.bitcast(BF16)[:, 0:1024], OUTF.bitcast(BF16)[:, 1024:2048]]
        MK = {}
        shared = {}

        def cvt_plain(src_h, dst_sc, n_strips, outw, tag, flsem):
            last = {}
            mfin = None
            for st in range(n_strips):
                b = st % 2
                lb = st % 3
                w = [last[("pe", st - 3)]] if ("pe", st - 3) in last else []
                src = src_h.rearrange("(s p) c -> s p c", p=128)[st]
                md = P.do("gpsimd", lambda e, s_=src, lb_=lb: e.dma_start(
                    PIN[lb_], s_), waits=w, dma="pin" + str(lb))
                pm = None
                for i in range(8):
                    wz = [md]
                    if i == 0 and ("ev", st - 2) in last:
                        wz.append(last[("ev", st - 2)])
                    pm = P.do("tensor", lambda e, i_=i, b_=b, lb_=lb: e.transpose(
                        [G0b, G1b][b_].rearrange("p (i c) -> p i c", c=128)[:, i_],
                        PIN[lb_].rearrange("p (i c) -> p i c", c=128)[:, i_],
                        IDENT[:, :]), waits=wz)
                last[("pe", st)] = pm
                wfl = [last[("fl", st - 2)]] if ("fl", st - 2) in last else []
                me = P.do("vector", lambda e, b_=b: e.tensor_copy(
                    PSTG[b_], [G0b, G1b][b_][:, 0:1024]), waits=[pm] + wfl)
                last[("ev", st)] = me
                mfin = P.do("gpsimd", lambda e, b_=b, st_=st: e.dma_start(
                    dst_sc.rearrange("p (i o) -> p i o", o=outw)
                    [:, :, st_ * 128: st_ * 128 + 128],
                    PSTG[b_].rearrange("p (i c) -> p i c", c=128)),
                    waits=[me], dma=flsem)
                last[("fl", st)] = mfin
            return mfin

        MK["wqkvT"] = cvt_plain(wqkv_h, sc_wqkvT, 24, 3 * D, "qkv", "flq")
        MK["woutT"] = cvt_plain(wout_h, sc_woutT, 8, D, "out", "flo")

        def cvt_conv():
            mfin = None
            prev_ev = None
            prev_flush = None
            pelast = {}
            for st in range(8):
                lb = st % 2
                src = wconv_h.rearrange("(s p) c k -> s p (c k)", p=128)[st]
                w = [pelast[st - 2]] if st >= 2 else []
                md = P.do("gpsimd", lambda e, s_=src, lb_=lb: e.dma_start(
                    PINC[lb_], s_), waits=w,
                    dma=("pinc" if lb == 0 else "pinc2"))
                for t_ in range(3):
                    pm = None
                    for i in range(8):
                        tin = PINC[lb].rearrange("p (i k) -> p i k", k=3)[:, :, t_] \
                            .rearrange("p (j c) -> p j c", c=128)[:, i]
                        wz = [md]
                        if i == 0 and prev_ev is not None:
                            wz.append(prev_ev)
                        pm = P.do("tensor", lambda e, tin_=tin, i_=i: e.transpose(
                            G0b.rearrange("p (i c) -> p i c", c=128)[:, i_],
                            tin_, IDENT[:, :]), waits=wz)
                    mprev_pe = pm
                    pelast[st] = pm
                    me = P.do("vector", lambda e: e.tensor_copy(PSTG[0], G0b[:, 0:1024]),
                              waits=[pm] + ([prev_flush] if prev_flush else []))
                    prev_ev = me
                    mfin = P.do("gpsimd", lambda e, st_=st, tt_=t_: e.dma_start(
                        sc_wconvT.rearrange("p (i o) -> p i o", o=D)
                        [:, tt_ * 8:(tt_ + 1) * 8, st_ * 128: st_ * 128 + 128],
                        PSTG[0].rearrange("p (i c) -> p i c", c=128)),
                        waits=[me], dma="flc")
                    prev_flush = mfin
            return mfin

        mprev_pe = None
        MK["wconvT"] = cvt_conv()
        MK3 = [MK["wqkvT"], MK["woutT"], MK["wconvT"]]
        for _eng in ("tensor", "scalar", "vector", "gpsimd"):
            P.do(_eng, lambda e: e.nop(), waits=MK3)
        P.close_stage()

        wq_pieces = []
        for (h_, d_) in ((wg_h, sc_wgT), (wu_h, sc_wuT)):
            for st in range(22):
                for pc_ in range(4):
                    wq_pieces.append((h_, d_, DFF, st, pc_ * 256, 256))
        for st in range(8):
            for pc_ in range(11):
                wq_pieces.append((wd_h, sc_wdT, D, st, pc_ * 256, 256))
        wqs = {"i": 0}

        def wq_quant():
            if wqs["i"] >= len(wq_pieces):
                return
            (src_h, dst_sc, outw, st, c0, L) = wq_pieces[wqs["i"]]
            wqs["i"] += 1
            nb = L // 16
            g = "vector"
            w = [wqs["q"]] if "q" in wqs else []
            md = P.do("gpsimd", lambda e, s_=src_h.rearrange("(s p) c -> s p c", p=128)[st]
                      [:, c0:c0 + L], L_=L: e.dma_start(WQF[:, 0:L_], s_),
                      waits=w, dma="wqe")
            wr = [wqs["rcp"]] if "rcp" in wqs else []
            m1 = P.do("vector", lambda e, L_=L, n_=nb: e.tensor_reduce(
                WQ_B[:, 0:n_], WQF[:, 0:L_].rearrange("p (a b) -> p a b", b=16),
                axis=AX.X, op=ALU.max, apply_absolute_value=True),
                waits=[md] + wr)
            m2 = P.do("vector", lambda e, n_=nb: e.tensor_scalar(
                WQ_S[:, 0:n_], WQ_B[:, 0:n_], C6, 1e-12, op0=ALU.mult, op1=ALU.max),
                waits=[m1])
            m3a = P.do("scalar", lambda e, n_=nb: e.activation(
                WQ_B[:, 0:n_], WQ_S[:, 0:n_], AF.Ln, bias=SM_ZERO), waits=[m2])
            m3 = P.do("scalar", lambda e, n_=nb: e.activation(
                WQ_R[:, 0:n_], WQ_B[:, 0:n_], AF.Exp, scale=-1.0), waits=[m3a])
            wqs["rcp"] = m3
            m4 = P.do(g, lambda e, L_=L, n_=nb: e.tensor_tensor(
                WQU[:, 0:L_].rearrange("p (a b) -> p a b", b=16),
                WQF[:, 0:L_].rearrange("p (a b) -> p a b", b=16),
                WQ_R[:, 0:n_].unsqueeze(2).broadcast_to([128, n_, 16]),
                op=ALU.mult), waits=[m3])
            sg1 = P.do(g, lambda e, L_=L: e.tensor_scalar(
                WQB[:, 256:256 + L_], WQU[:, 0:L_], 0.0, -2.0,
                op0=ALU.is_lt, op1=ALU.mult), waits=[m4])
            sg2 = P.do(g, lambda e, L_=L: e.tensor_scalar(
                WQB[:, 256:256 + L_], WQB[:, 256:256 + L_], 1.0, None,
                op0=ALU.add), waits=[sg1])
            m5 = P.do(g, lambda e, L_=L: e.tensor_tensor(
                WQF[:, 0:L_], WQU[:, 0:L_], WQB[:, 256:256 + L_], op=ALU.mult),
                waits=[sg2])
            f1 = P.do(g, lambda e, L_=L: e.tensor_scalar(
                WQB[:, 0:L_], WQF[:, 0:L_], 64.0, None, op0=ALU.add), waits=[m5])
            a1 = P.do(g, lambda e, L_=L: e.tensor_scalar(
                WQB[:, 0:L_], WQB[:, 0:L_], 64.0, 2.0, op0=ALU.subtract, op1=ALU.min),
                waits=[f1])
            wtp = [wqs["tpm"]] if "tpm" in wqs else []
            f2 = P.do(g, lambda e, L_=L: e.tensor_scalar(
                WQO[:, 0:L_], WQF[:, 0:L_], 128.0, None, op0=ALU.add),
                waits=[a1] + wtp)
            b1 = P.do(g, lambda e, L_=L: e.tensor_scalar(
                WQO[:, 0:L_], WQO[:, 0:L_], 130.0, 0.0, op0=ALU.subtract, op1=ALU.max),
                waits=[f2])
            g0a = P.do(g, lambda e, L_=L: e.tensor_scalar(
                WQO[:, 0:L_], WQO[:, 0:L_], 2.0, None, op0=ALU.min), waits=[b1])
            g0 = P.do(g, lambda e, L_=L: e.tensor_tensor(
                WQO[:, 0:L_], WQO[:, 0:L_], WQB[:, 0:L_], op=ALU.add), waits=[g0a])
            c1 = P.do(g, lambda e, L_=L: e.tensor_scalar(
                WQB[:, 0:L_], WQF[:, 0:L_], 5.0, 2.0, op0=ALU.is_gt, op1=ALU.mult),
                waits=[g0])
            g2 = P.do(g, lambda e, L_=L: e.tensor_tensor(
                WQO[:, 0:L_], WQO[:, 0:L_], WQB[:, 0:L_], op=ALU.add), waits=[c1])
            t1 = P.do(g, lambda e, L_=L: e.tensor_tensor(
                WQO[:, 0:L_], WQO[:, 0:L_], WQB[:, 256:256 + L_], op=ALU.mult),
                waits=[g2, sg2])
            q = P.do(g, lambda e, L_=L, n_=nb: e.tensor_tensor(
                WQO[:, 0:L_].rearrange("p (a b) -> p a b", b=16),
                WQO[:, 0:L_].rearrange("p (a b) -> p a b", b=16),
                WQ_S[:, 0:n_].unsqueeze(2).broadcast_to([128, n_, 16]),
                op=ALU.mult), waits=[t1])
            wqs["q"] = q
            wqs["pend"] = (dst_sc, outw, st, c0, L, q)

        def wq_flush(att=None):
            if "pend" not in wqs:
                return
            (dst_sc, outw, st, c0, L, q) = wqs.pop("pend")
            nblk = L // 128
            if att is None:
                tb_ap, tkey, extw = CTEb, 6, []
            else:
                rg_a, em_a = att
                tb_ap = PS8b[:, rg_a * 2048: rg_a * 2048 + 2048]
                tkey = 2 * rg_a
                extw = [em_a]
            pm = None
            for b in range(nblk):
                wz = [q]
                if b == 0:
                    wz += wb(tkey) + extw + ([wqs["tf"]] if "tf" in wqs else [])
                pm = P.do("tensor", lambda e, b_=b, a_=tb_ap: e.transpose(
                    a_.rearrange("p (i c) -> p i c", c=128)[:, b_],
                    WQO.rearrange("p (i c) -> p i c", c=128)[:, b_],
                    IDENT[:, :]), waits=wz)
            wqs["tpm"] = pm
            me = P.do("vector", lambda e, L_=L, a_=tb_ap: e.tensor_copy(
                WQT[:, 0:L_], a_[:, 0:L_]), waits=[pm])
            pfree[tkey] = me
            mf = P.do("sync", lambda e, d_=dst_sc, ow=outw, st_=st, c0_=c0, n_=nblk:
                      e.dma_start(
                          d_.rearrange("p (i o) -> p i o", o=ow)
                          [:, c0_ // 128: c0_ // 128 + n_, st_ * 128: st_ * 128 + 128],
                          WQT[:, 0:n_ * 128].rearrange("p (i c) -> p i c", c=128)),
                      waits=[me], dma="wqf")
            wqs["tf"] = mf

        lh = {"slot": 0, "free": {}}

        def lhsp_load(src_ap, waits=()):
            sl = lh["slot"]
            lh["slot"] = (sl + 1) % 4
            w = list(waits) + ([lh["free"][sl]] if sl in lh["free"] else [])
            m = P.do("sync", lambda e, s_=src_ap, sl_=sl: e.dma_start(
                LHSP.rearrange("p (s c) -> p s c", c=1024)[:, sl_], s_),
                waits=w, dma="lh" + str(sl))
            return sl, m

        rh = {"slot": 0, "free": {}}

        def rhsp_load(src_ap, waits=()):
            sl = rh["slot"]
            rh["slot"] = (sl + 1) % 2
            w = list(waits) + ([rh["free"][sl]] if sl in rh["free"] else [])
            m = P.do("sync", lambda e, s_=src_ap, sl_=sl: e.dma_start(
                RHSP.rearrange("p (s c) -> p s c", c=512)[:, sl_], s_),
                waits=w, dma="rh" + str(sl))
            return sl, m

        def emit_xt_qkv(c):
            xst = {}
            mtr = []
            q1g = [attn_state["qlast"]] if "qlast" in attn_state else []
            B01 = [PS8b[:, 0:1024], PS8b[:, 1024:2048]]
            for r in range(4):
                w = [xst[r - 1]] if (r - 1) in xst else (
                    [shared["xtmlast"]] if "xtmlast" in shared else [])
                md = P.do("gpsimd", lambda e, r_=r: e.dma_start(
                    XTM[:, 0:1024],
                    x_h.rearrange("(s p) c -> s p c", p=128)[4 * c + r_]),
                    waits=w, dma="xm")
                pm = None
                for i in range(8):
                    wz = [md] + (wb(r % 2) if i == 0 else [])
                    pm = P.do("tensor", lambda e, i_=i, a_=B01[r % 2]: e.transpose(
                        a_.rearrange("p (i c) -> p i c", c=128)[:, i_],
                        XTM[:, 0:1024].rearrange("p (i c) -> p i c", c=128)[:, i_],
                        IDENT[:, :]), waits=wz)
                xst[r] = pm
                me = P.do("vector", lambda e, r_=r, a_=B01[r % 2]: e.tensor_copy(
                    XT.rearrange("p (i c) -> p i c", c=512)[:, :, r_ * 128:(r_ + 1) * 128],
                    a_.rearrange("p (i c) -> p i c", c=128)),
                    waits=[pm])
                pfree[bkey(r % 2)] = me
                mtr.append(me)
            for o in range(16):
                sl, mw = lhsp_load(
                    sc_wqkvT.rearrange("p (i o) -> p i o", o=3 * D)
                    [:, :, o * 128:(o + 1) * 128], waits=MK3)
                bi = 4 + (o % 2)
                pm = None
                for i in range(8):
                    wz = [mw]
                    if i == 0:
                        wz += wb(bi) + mtr
                    pm = P.do("tensor", lambda e, i_=i, sl_=sl, bi_=bi: e.matmul(
                        bank_ap(bi_),
                        LHSP.rearrange("p (s c) -> p s c", c=1024)[:, sl_]
                        .rearrange("p (i c) -> p i c", c=128)[:, i_],
                        XT[:, i_ * 512:(i_ + 1) * 512],
                        start=(i_ == 0), stop=(i_ == 7), skip_group_check=True),
                        waits=wz)
                lh["free"][sl] = pm
                if o < 8:
                    dst = Q1[:, o * 512:(o + 1) * 512]
                else:
                    dst = KT[:, (o - 8) * S + c * 512: (o - 8) * S + c * 512 + 512]
                me = P.do("scalar", lambda e, d_=dst, bi_=bi, o_=o: e.activation(
                    d_, bank_ap(bi_), AF.Identity, bias=SM_BQKV[:, o_:o_ + 1]),
                    waits=[pm] + q1g)
                pfree[bkey(bi)] = me
                shared["qkvevac"] = me
                mar = P.do("vector", lambda e, d_=dst: e.tensor_reduce(
                    SM_RED, d_, axis=AX.X, op=ALU.max,
                    apply_absolute_value=True),
                    waits=[me] + ([shared[("accrd", 1)]]
                                  if o == 0 and ("accrd", 1) in shared else []))
                mac = P.do("vector", lambda e, t_=(SM_QAP if o < 8 else SM_KAP):
                           e.tensor_tensor(t_, t_, SM_RED, op=ALU.max),
                           waits=[mar])
                shared["qaccl"] = mac
                if c == 0 and o == 0:
                    tap("lhs0", LHSP[:, 0:1024], waits=[pm])
            for hv in range(2):
                first = [True] * 4
                for i in range(8):
                    sl, mw = rhsp_load(
                        sc_wqkvT.rearrange("p (i o) -> p i o", o=3 * D)
                        [:, i, 2048 + hv * 512: 2048 + hv * 512 + 512],
                        waits=MK3)
                    for r in range(4):
                        wz = [mw]
                        if first[r]:
                            wz += wb(r) + [mtr[r]]
                            first[r] = False
                        pm = P.do("tensor", lambda e, i_=i, sl_=sl, r_=r: e.matmul(
                            bank_ap(r_),
                            XT[:, i_ * 512 + r_ * 128: i_ * 512 + r_ * 128 + 128],
                            RHSP.rearrange("p (s c) -> p s c", c=512)[:, sl_],
                            start=(i_ == 0), stop=False, skip_group_check=True),
                            waits=wz)
                        rh["free"][sl] = pm
                mw = P.do("gpsimd", lambda e, hv_=hv: e.dma_start(
                    BCS[0:1, hv_ * 512:(hv_ + 1) * 512],
                    bqkv_h.rearrange("(a c) -> a c", c=512)[4 + hv_: 5 + hv_, :]),
                    waits=([shared["bcslast"]] if "bcslast" in shared else []),
                    dma="vb")
                for r in range(4):
                    pm = P.do("tensor", lambda e, hv_=hv, r_=r: e.matmul(
                        bank_ap(r_), ONESB[0:1, 0:128],
                        BCS[0:1, hv_ * 512:(hv_ + 1) * 512],
                        start=False, stop=True, skip_group_check=True),
                        waits=[mw])
                    me = P.do("vector", lambda e, r_=r, hv_=hv, c_=c: e.tensor_copy(
                        V.rearrange("p (s h d) -> p s h d", h=H, d=65)
                        [:, 4 * c_ + r_, hv_ * 8:(hv_ + 1) * 8, 0:64],
                        bank_ap(r_).rearrange("p (h d) -> p h d", d=64)),
                        waits=[pm])
                    pfree[bkey(r)] = me
                    shared["vevac"] = me
            shared["xtmlast"] = xst[3]

        def emit_bias(c, which, q_src, jw=()):
            if which == 1:
                specs = [(SM_QAP, SM_QA), (SM_KAP, SM_KA)]
            else:
                specs = [(SM_TAP, SM_T0)]
            upd = []
            prev_pm = None
            for (srcp, dst) in specs:
                wz0 = ([prev_pm] if prev_pm else []) + list(jw) + wb(6)
                pm = P.do("tensor", lambda e, s_=srcp: e.transpose(
                    G0[0:1, 0:128], s_, IDENTF[:, :]), waits=wz0)
                prev_pm = pm
                shared[("accrd", which)] = pm
                mr = P.do("vector", lambda e: e.tensor_reduce(
                    SM_RED2, G0[0:1, 0:128], axis=AX.X, op=ALU.max), waits=[pm])
                pfree[6] = mr
                upd.append(P.do("vector", lambda e, d_=dst: e.tensor_copy(
                    d_, SM_RED2), waits=[mr]))
            qa = SM_QA if which == 1 else SM_T0
            m1 = P.do("vector", lambda e, q_=qa: e.tensor_tensor(
                SM_CV, q_, SM_KA, op=ALU.mult), waits=upd)
            m2 = P.do("vector", lambda e: e.tensor_scalar(
                SM_CV, SM_CV, -8.0, -30.0, op0=ALU.mult, op1=ALU.max),
                waits=[m1])
            pm = P.do("tensor", lambda e: e.matmul(
                G0[:, 0:1], ONESF[0:1, 0:128], SM_CV, start=True, stop=True,
                skip_group_check=True), waits=[m2] + wb(6))
            mb = P.do("vector", lambda e: e.tensor_copy(SM_EB, G0[:, 0:1]),
                      waits=[pm] + ([attn_state["explast"]]
                                    if "explast" in attn_state else []))
            pfree[6] = mb
            shared["eblast"] = mb

        attn_state = {"rg": 0, "epi": None}
        pslot_free = {}

        def emit_attn(c, which, q_src):
            nch = 4 * (c + 1)
            NR = 3
            LOOK = 2
            inj = set(range(2, nch - 5, 4))
            perm = list(range(4 * c, nch)) + list(range(0, 4 * c))
            for hp in range(8):
                rg0 = attn_state["rg"]
                exp_by_region = {}
                ctx_first = {0: True, 1: True}
                lastctx = {0: None, 1: None}
                expm = {}
                maskm = {}

                def emit_scores(k, ebr=exp_by_region, expm_=expm, maskm_=maskm,
                                rg0_=rg0, hp_=hp):
                    j = perm[k]
                    rg = (rg0_ + k) % NR
                    mdiag = j - 4 * c
                    q0 = max(mdiag, 0) * 128
                    mm = None
                    for half in range(2):
                        wz = list(wb(2 * rg + half))
                        if k >= NR and half == 0:
                            wz.append(ebr[rg])
                        mm = P.do("tensor", lambda e, j_=j, h_=half, hq_=hp_,
                                  q0_=q0, rg_=rg: e.matmul(
                                      PS8[:, rg_ * 1024 + h_ * 512 + q0_:
                                          rg_ * 1024 + (h_ + 1) * 512],KT[64 * h_:64 * (h_ + 1),
                                         hq_ * S + j_ * 128: hq_ * S + (j_ + 1) * 128],
                                      q_src[64 * h_:64 * (h_ + 1),
                                            hq_ * 512 + q0_: (hq_ + 1) * 512],
                                      start=True, stop=True,
                                      tile_position=(64 * h_, 0),
                                      skip_group_check=True), waits=wz)
                    attn_state["qlast"] = mm
                    sl = k % 4
                    wz = [mm]
                    if sl in pslot_free:
                        wz.append(pslot_free[sl])
                    if k == 0 and hp_ == 0 and "eblast" in shared:
                        wz.append(shared["eblast"])
                    if mdiag >= 2:
                        for half in range(2):
                            me_ = P.do("scalar", lambda e, sl_=sl, rg_=rg,
                                       h_=half, q0_=q0: e.activation(
                                           PBUF[:, sl_ * 1024 + h_ * 512 + q0_:
                                                sl_ * 1024 + (h_ + 1) * 512],
                                           PS8[:, rg_ * 1024 + h_ * 512 + q0_:
                                               rg_ * 1024 + (h_ + 1) * 512],
                                           AF.Exp, bias=SM_EB, scale=0.125),
                                       waits=wz)
                    else:
                        me_ = P.do("scalar", lambda e, sl_=sl, rg_=rg: e.activation(
                            PBUF[:, sl_ * 1024:(sl_ + 1) * 1024],
                            PS8[:, rg_ * 1024:(rg_ + 1) * 1024], AF.Exp,
                            bias=SM_EB, scale=0.125), waits=wz)
                    ebr[rg] = me_
                    expm_[k] = me_
                    attn_state["explast"] = me_
                    if mdiag >= 0:
                        for half in range(2):
                            off = sl * 1024 + half * 512 + mdiag * 128
                            maskm_[(k, half)] = P.do(
                                "vector", lambda e, o_=off: e.tensor_tensor(
                                    PBUF[:, o_: o_ + 128],
                                    PBUF[:, o_: o_ + 128],
                                    TRI[:, :], op=ALU.mult), waits=[me_])

                def emit_ctx(k, expm_=expm, maskm_=maskm, ctxf=ctx_first,
                             lc=lastctx, hp_=hp):
                    j = perm[k]
                    mdiag = j - 4 * c
                    q0 = max(mdiag, 0) * 128
                    sl = k % 4
                    for half in range(2):
                        head = 2 * hp_ + half
                        wz = [maskm_.get((k, half), expm_[k])]
                        if ctxf[half]:
                            wz += wb(6 + half)
                            ctxf[half] = False
                            st_flag = True
                        else:
                            st_flag = False
                        lc[half] = P.do(
                            "tensor", lambda e, j_=j, h_=half, hd_=head, sl_=sl,
                            q0_=q0, st_=st_flag: e.matmul(
                                [CTE, CTO][h_][0:65, q0_:512],
                                V[:, j_ * 1040 + hd_ * 65:
                                   j_ * 1040 + hd_ * 65 + 65],
                                PBUF[:, sl_ * 1024 + h_ * 512 + q0_:
                                     sl_ * 1024 + (h_ + 1) * 512],
                                start=st_, stop=False,
                                skip_group_check=True), waits=wz)
                    pslot_free[sl] = lc[1]

                for k in range(nch):
                    emit_scores(k)
                    if k == 1 and attn_state["epi"] is not None:
                        attn_state["epi"]()
                        attn_state["epi"] = None
                    if k >= LOOK:
                        emit_ctx(k - LOOK)
                    if k in inj:
                        wq_flush(att=((rg0 + k) % NR, expm[k]))
                        wq_quant()
                for k in range(max(nch - LOOK, 0), nch):
                    emit_ctx(k)

                rg_last = (rg0 + nch - 1) % NR
                for rr in range(NR):
                    if rr != rg_last:
                        pfree[2 * rr] = exp_by_region[rr]
                        pfree[2 * rr + 1] = exp_by_region[rr]
                attn_state["rg"] = (rg_last + 1) % NR
                if which == 1:
                    dst = CTXW.rearrange("p (b k x) -> p b k x", b=2, x=514)[
                        :, c % 2, hp, 1:513]
                else:
                    dst = CTX2[:, hp * 512:(hp + 1) * 512]

                def make_epi(rgl=rg_last, lc0=lastctx[0], lc1=lastctx[1],
                             expl=expm[nch - 1], d_=dst):
                    def epi():
                        ml_ = P.do("scalar", lambda e: e.activation(
                            LNSTG, PS8[64:65, 3072:4096], AF.Ln,
                            bias=SM_ZERO[0:1, :]),
                            waits=[lc0, lc1, expl]
                            + attn_state.get("evprev", [])
                            + ([shared["sgnlast"]]
                               if "sgnlast" in shared else []))
                        mr = P.do("scalar", lambda e: lp(
                            lambda: e.activation(
                                BCS[0:1, 0:1024],
                                LNSTG, AF.Exp, scale=-1.0)), waits=[ml_])
                        mrc = [mr, mr]
                        bc = []
                        for half in range(2):
                            pm = P.do("tensor", lambda e, h_=half, rg_=rgl:
                                      e.matmul(
                                          PS8[0:64, rg_ * 1024 + h_ * 512:
                                              rg_ * 1024 + h_ * 512 + 512],
                                          ONESB[0:1, 0:64],
                                          BCS[0:1, h_ * 512: h_ * 512 + 512],
                                          start=True, stop=True,
                                          skip_group_check=True),
                                      waits=[mrc[half]])
                            bc.append(pm)
                        mbc = P.do("scalar", lambda e, rg_=rgl: e.activation(
                            BCS[0:64, 0:1024],
                            PS8[0:64, rg_ * 1024:(rg_ + 1) * 1024],
                            AF.Identity, bias=SM_ZERO[0:64, :]), waits=bc)
                        pfree[2 * rgl] = mbc
                        pfree[2 * rgl + 1] = mbc
                        mev = P.do("vector", lambda e, dd_=d_: e.tensor_tensor(
                            dd_[0:64, :], CTE[0:64, 0:512], BCS[0:64, 0:512],
                            op=ALU.mult), waits=[mbc])
                        mno = P.do("vector", lambda e: e.tensor_tensor(
                            TMPO[0:64, :], CTO[0:64, 0:512],
                            BCS[0:64, 512:1024], op=ALU.mult), waits=[mbc])
                        pfree[7] = mno
                        psh = P.do("tensor", lambda e: e.matmul(
                            PS8[64:128, 3072:3584],
                            IDENT[0:64, 0:64], TMPO[0:64, :],
                            start=True, stop=True, tile_position=(0, 64),
                            skip_group_check=True), waits=[mev, mno])
                        mod = P.do("scalar", lambda e, dd_=d_:
                                   e.activation(
                                       dd_[64:128, :],
                                       PS8[64:128, 3072:3584],
                                       AF.Identity, bias=SM_ZERO[64:128, :]),
                                   waits=[psh])
                        pfree[6] = mod
                        attn_state["evprev"] = [mev, mno]
                        attn_state["ctxlast"] = mod
                    return epi

                attn_state["epi"] = make_epi()
            attn_state["epi"]()
            attn_state["epi"] = None

        def emit_halo(c):
            cb = c % 2
            hw = [attn_state["ctxlast"]] + (
                [shared["convlast"]] if "convlast" in shared else [])
            hm = []
            if c == 0:
                hm.append(P.do("gpsimd", lambda e: e.memset(
                    CTXW.rearrange("p (b k x) -> p b k x", b=2, x=514)
                    [:, 0, :, 0:1], 0.0), waits=hw))
            else:
                hm.append(P.do("gpsimd", lambda e, cb_=cb: e.tensor_copy(
                    CTXW.rearrange("p (b k x) -> p b k x", b=2, x=514)[:, cb_, :, 0:1],
                    CTXW.rearrange("p (b k x) -> p b k x", b=2, x=514)
                    [:, 1 - cb_, :, 512:513]), waits=hw))
                hm.append(P.do("gpsimd", lambda e, cb_=cb: e.tensor_copy(
                    CTXW.rearrange("p (b k x) -> p b k x", b=2, x=514)
                    [:, 1 - cb_, :, 513:514],
                    CTXW.rearrange("p (b k x) -> p b k x", b=2, x=514)
                    [:, cb_, :, 1:2]), waits=hw))
            return hm

        def emit_conv(t, hm=(), serial=False):
            tb = t % 2
            q1g = [attn_state["qlast"]] if "qlast" in attn_state else []
            for o in range(8):
                bi = o % 4
                slots = []
                prevw = [MK["wconvT"]]
                for t_ in range(3):
                    sl, mw = lhsp_load(
                        sc_wconvT.rearrange("p (i o) -> p i o", o=D)
                        [:, t_ * 8:(t_ + 1) * 8, o * 128:(o + 1) * 128],
                        waits=prevw)
                    if serial:
                        prevw = [mw]
                    slots.append((sl, mw))
                if serial and o == 0 and t == n_tiles - 1:
                    tap("cvw0", LHSP[:, 0:4096], waits=[m for (_s, m) in slots])
                pm = None
                n_mm = 0
                for t_ in range(3):
                    sl, mw = slots[t_]
                    for i in range(8):
                        wz = [mw]
                        if n_mm == 0:
                            wz += wb(bi) + list(hm)
                        off = tb * 4112 + i * 514 + t_
                        pm = P.do("tensor", lambda e, i_=i, sl_=sl, bi_=bi,
                                  off_=off, st0=(n_mm == 0), sp0=(n_mm == 23):
                                  e.matmul(
                                      bank_ap(bi_),
                                      LHSP[:, sl_ * 1024 + i_ * 128:
                                           sl_ * 1024 + i_ * 128 + 128],
                                      CTXW[:, off_: off_ + 512],
                                      start=st0, stop=sp0,
                                      skip_group_check=True), waits=wz)
                        n_mm += 1
                    lh["free"][sl] = pm
                wz_me = [pm]
                if serial and o == 0 and t == n_tiles - 1:
                    mdbg = P.do("vector", lambda e, bi_=bi: e.tensor_copy(
                        XQO.bitcast(F32)[:, 0:512], bank_ap(bi_)), waits=[pm])
                    tap("cvps0", XQO.bitcast(F32)[:, 0:512], waits=[mdbg])
                    wz_me.append(mdbg)
                me = P.do("scalar", lambda e, bi_=bi, o_=o: e.activation(
                    Q1[:, o_ * 512:(o_ + 1) * 512], bank_ap(bi_), AF.Identity,
                    bias=SM_BCONV[:, o_:o_ + 1]), waits=wz_me + q1g)
                pfree[bkey(bi)] = me
                shared["convevac"] = me
                shared["convlast"] = pm
                mar = P.do("vector", lambda e, o_=o: e.tensor_reduce(
                    SM_RED, Q1[:, o_ * 512:(o_ + 1) * 512], axis=AX.X,
                    op=ALU.max, apply_absolute_value=True),
                    waits=[me] + ([shared[("accrd", 2)]]
                                  if o == 0 and ("accrd", 2) in shared else []))
                if o == 0:
                    mac = P.do("vector", lambda e: e.tensor_copy(
                        SM_TAP, SM_RED), waits=[mar])
                else:
                    mac = P.do("vector", lambda e: e.tensor_tensor(
                        SM_TAP, SM_TAP, SM_RED, op=ALU.max), waits=[mar])
                shared["taccl"] = mac
                if serial:
                    P.close_stage()

        def emit_outproj_ln_quant(t):
            xload = {}
            st_ = shared
            for r in range(2):
                xload[r] = P.do("sync", lambda e, r_=r: e.dma_start(
                    XF.rearrange("p (b c) -> p b c", c=1024)[:, r_ % 2],
                    x_h.rearrange("(s p) c -> s p c", p=128)[4 * t + r_]),
                    waits=(MK3 if t == 0 else [])
                    + ([st_[("xfg", r % 2)]] if ("xfg", r % 2) in st_ else []),
                    dma="xf" + str(r % 2))
            mw_b = P.do("gpsimd", lambda e: e.dma_start(
                BCS[0:1, 0:1024], bout_h.rearrange("(a c) -> a c", c=1024)),
                waits=[attn_state["ctxlast"]], dma="vb")
            strip_done = {}
            for half in range(2):
                first = [True] * 4
                for i in range(8):
                    slw, mww = rhsp_load(
                        sc_woutT.rearrange("p (i o) -> p i o", o=D)
                        [:, i, half * 512:(half + 1) * 512],
                        waits=[MK["woutT"]])
                    for r in range(4):
                        bi = half * 4 + r
                        wz = [mww]
                        if first[r]:
                            wz += wb(bi) + [attn_state["ctxlast"]]
                            first[r] = False
                        pm = P.do("tensor", lambda e, i_=i, sl_=slw, r_=r, bi_=bi:
                                  e.matmul(
                                      bank_ap(bi_),
                                      CTX2[:, i_ * 512 + r_ * 128:
                                           i_ * 512 + r_ * 128 + 128],
                                      RHSP.rearrange("p (s c) -> p s c", c=512)[:, sl_],
                                      start=(i_ == 0), stop=False,
                                      skip_group_check=True), waits=wz)
                        rh["free"][slw] = pm
                for r in range(4):
                    bi = half * 4 + r
                    pm = P.do("tensor", lambda e, h_=half, bi_=bi: e.matmul(
                        bank_ap(bi_), ONESB[0:1, 0:128],
                        BCS[0:1, h_ * 512:(h_ + 1) * 512],
                        start=False, stop=True, skip_group_check=True),
                        waits=[mw_b])
                    strip_done[(r, half)] = pm
                    shared["bcslast"] = pm
            for r in range(4):
                b = r % 2
                madd = None
                for half in range(2):
                    bi = half * 4 + r
                    madd = P.do("vector", lambda e, h_=half, b_=b, bi_=bi:
                                e.tensor_tensor(
                                    XF.rearrange("p (b c) -> p b c", c=1024)
                                    [:, b_, h_ * 512:(h_ + 1) * 512],
                                    bank_ap(bi_),
                                    XF.rearrange("p (b c) -> p b c", c=1024)
                                    [:, b_, h_ * 512:(h_ + 1) * 512], op=ALU.add),
                                waits=[strip_done[(r, half)], xload[r]])
                    pfree[bkey(bi)] = madd
                y1 = XF.rearrange("p (b c) -> p b c", c=1024)[:, b]
                ms = P.do("vector", lambda e, y_=y1: e.reduce_sum(
                    SM_R0, y_, axis=AX.X), waits=[madd])
                mq = P.do("vector", lambda e, y_=y1: e.scalar_tensor_tensor(
                    XQO[:, 0:1024], y_, 0.0, y_, op0=ALU.bypass, op1=ALU.mult,
                    accum_out=SM_R1),
                    waits=[madd] + ([st_[("xqdma", r - 1)]] if ("xqdma", r - 1) in st_ else []))
                mm_ = P.do("vector", lambda e: e.tensor_scalar(
                    SM_R2, SM_R0, 1.0 / 1024.0, None, op0=ALU.mult), waits=[ms])
                mv1 = P.do("vector", lambda e: e.tensor_scalar(
                    SM_R3, SM_R1, 1.0 / 1024.0, None, op0=ALU.mult), waits=[mq])
                mv2 = P.do("vector", lambda e: e.tensor_tensor(
                    SM_R4, SM_R2, SM_R2, op=ALU.mult), waits=[mm_])
                mv3 = P.do("vector", lambda e: e.tensor_tensor(
                    SM_R3, SM_R3, SM_R4, op=ALU.subtract), waits=[mv1, mv2])
                ml1 = P.do("scalar", lambda e: e.activation(
                    SM_R4, SM_R3, AF.Ln, bias=SM_EPS), waits=[mv3])
                ml2 = P.do("scalar", lambda e: e.activation(
                    SM_R3, SM_R4, AF.Exp, scale=-0.5), waits=[ml1])
                mn = P.do("vector", lambda e, y_=y1: e.tensor_scalar(
                    y_, y_, SM_R2, SM_R3, op0=ALU.subtract, op1=ALU.mult),
                    waits=[ml2])
                mst = P.do("scalar", lambda e, r_=r, y_=y1: e.dma_start(
                    sc_y1.rearrange("(s p) c -> s p c", p=128)[4 * t + r_], y_),
                    waits=[mn], dma="dy")
                shared["y1_last"] = mst
                msg1 = P.do("vector", lambda e, y_=y1: e.tensor_scalar(
                    XQT1[:, :], y_, 0.0, -2.0, op0=ALU.is_lt, op1=ALU.mult),
                    waits=[mn])
                msg2 = P.do("vector", lambda e: e.tensor_scalar(
                    XQT1[:, :], XQT1[:, :], 1.0, None, op0=ALU.add), waits=[msg1])
                mb1 = P.do("vector", lambda e, y_=y1: e.tensor_reduce(
                    SM_XB, y_.rearrange("p (a b) -> p a b", b=16), axis=AX.X,
                    op=ALU.max, apply_absolute_value=True),
                    waits=[mn] + ([st_["xrcp"]] if "xrcp" in st_ else []))
                mb2 = P.do("vector", lambda e: e.tensor_scalar(
                    SM_XS, SM_XB, C6, 1e-12, op0=ALU.mult, op1=ALU.max), waits=[mb1])
                mb3a = P.do("scalar", lambda e: e.activation(
                    SM_XB, SM_XS, AF.Ln, bias=SM_ZERO), waits=[mb2])
                mb3 = P.do("scalar", lambda e: e.activation(
                    SM_XR, SM_XB, AF.Exp, scale=-1.0), waits=[mb3a])
                st_["xrcp"] = mb3
                mu1 = P.do("vector", lambda e, y_=y1: e.tensor_scalar(
                    y_.bitcast(I32), y_.bitcast(I32), 0x7FFFFFFF, None,
                    op0=ALU.bitwise_and), waits=[msg1, mst, mb1])
                mu2 = P.do("vector", lambda e, y_=y1: e.tensor_tensor(
                    y_.rearrange("p (a b) -> p a b", b=16),
                    y_.rearrange("p (a b) -> p a b", b=16),
                    SM_XR.unsqueeze(2).broadcast_to([128, 64, 16]), op=ALU.mult),
                    waits=[mu1, mb3])
                mf1 = P.do("vector", lambda e, y_=y1: e.tensor_scalar(
                    XQT2[:, :], y_, 64.0, None, op0=ALU.add), waits=[mu2])
                ma1 = P.do("vector", lambda e: e.tensor_scalar(
                    XQT2[:, :], XQT2[:, :], 64.0, 2.0, op0=ALU.subtract,
                    op1=ALU.min), waits=[mf1])
                mf2 = P.do("vector", lambda e, y_=y1: e.tensor_scalar(
                    XQO[:, :], y_, 128.0, None, op0=ALU.add), waits=[ma1])
                mb1_ = P.do("vector", lambda e: e.tensor_scalar(
                    XQO[:, :], XQO[:, :], 130.0, 0.0, op0=ALU.subtract,
                    op1=ALU.max), waits=[mf2])
                mg0 = P.do("vector", lambda e: e.scalar_tensor_tensor(
                    XQO[:, :], XQO[:, :], 2.0, XQT2[:, :], op0=ALU.min,
                    op1=ALU.add), waits=[mb1_])
                mc1 = P.do("vector", lambda e, y_=y1: e.tensor_scalar(
                    XQT2[:, :], y_, 5.0, 2.0, op0=ALU.is_gt, op1=ALU.mult),
                    waits=[mg0])
                st_[("xfg", r % 2)] = mc1
                if r + 2 < 4:
                    xload[r + 2] = P.do("sync", lambda e, r_=r + 2: e.dma_start(
                        XF.rearrange("p (b c) -> p b c", c=1024)[:, r_ % 2],
                        x_h.rearrange("(s p) c -> s p c", p=128)[4 * t + r_]),
                        waits=[mc1], dma="xf" + str(r % 2))
                mg2 = P.do("vector", lambda e: e.tensor_tensor(
                    XQO[:, :], XQO[:, :], XQT2[:, :], op=ALU.add), waits=[mc1])
                mt1 = P.do("vector", lambda e: e.tensor_tensor(
                    XQO[:, :], XQO[:, :], XQT1[:, :], op=ALU.mult),
                    waits=[mg2, msg2])
                st_["sgnlast"] = mt1
                mqf = P.do("vector", lambda e: e.tensor_tensor(
                    XQO.rearrange("p (a b) -> p a b", b=16),
                    XQO.rearrange("p (a b) -> p a b", b=16),
                    SM_XS.unsqueeze(2).broadcast_to([128, 64, 16]), op=ALU.mult),
                    waits=[mt1])
                st_[("xqdma", r)] = P.do("scalar", lambda e, r_=r: e.dma_start(
                    sc_xq.rearrange("(s p) c -> s p c", p=128)[4 * t + r_],
                    XQO[:, :]), waits=[mqf], dma="dxq")

        for c in range(n_tiles):
            emit_xt_qkv(c)
            if c == 0:
                tap("wsc", sc_wqkvT)
                tap("wcc", sc_wconvT)
                tap("xt0", XT[:, :])
                tap("q10", Q1[:, :])
                tap("kt0", KT[:, :])
                tap("v0", V[:, :])
            emit_bias(c, 1, Q1,
                      jw=[shared["qaccl"], shared["vevac"]])
            if c == 0:
                tap("eb0", SM[:, 0:64])
            emit_attn(c, 1, Q1)
            if c == 0:
                tap("ctx0", CTXW[:, :])
            hm_halo = emit_halo(c)
            if c >= 1:
                emit_conv(c - 1, hm=hm_halo)
                if c == 1:
                    tap("q20", Q1[:, :])
                emit_bias(c - 1, 2, Q1, jw=[shared["taccl"]])
                emit_attn(c - 1, 2, Q1)
                if c == 1:
                    tap("ctx20", CTX2[:, :])
                emit_outproj_ln_quant(c - 1)
                if c == 1:
                    tap("xq0", XQO[:, :])
        tl = n_tiles - 1
        mts = P.do("gpsimd", lambda e: e.memset(
            CTXW.rearrange("p (b k x) -> p b k x", b=2, x=514)
            [:, tl % 2, :, 513:514], 0.0),
            waits=[attn_state["ctxlast"]]
            + ([shared["convlast"]] if "convlast" in shared else []))
        emit_conv(tl, hm=[mts], serial=DEV_SERIAL_CONV)
        tap("q2t", Q1[:, :])
        emit_bias(tl, 2, Q1, jw=[shared["taccl"]])
        tap("eb2", SM[:, 0:64])
        emit_attn(tl, 2, Q1)
        tap("ctx2t", CTX2[:, :])
        emit_outproj_ln_quant(tl)
        tap("xqsc", sc_xq, waits=list(P.fence) + [shared[("xqdma", 3)]])
        tap("y1sc", sc_y1, waits=list(P.fence) + [shared["y1_last"]])
        while wqs["i"] < len(wq_pieces):
            wq_flush()
            wq_quant()
        wq_flush()
        P.close_stage()

        if do_pc:
            pbf = list(P.fence)
            mwg = P.do("sync", lambda e: e.dma_start(WG[:, :], sc_wgT),
                       waits=pbf + [wqs["tf"]], dma="wgt")
            mwu = P.do("sync", lambda e: e.dma_start(WU[:, :], sc_wuT),
                       waits=pbf + [wqs["tf"]], dma="wgu")
            mwd = P.do("sync", lambda e: e.dma_start(WD[:, :], sc_wdT),
                       waits=pbf + [wqs["tf"]], dma="wgd")
            pcg = {}

            def emit_xqst(c):
                B45 = [PS8b[:, 4096:5120], PS8b[:, 5120:6144]]
                mtr = []
                for r in range(4):
                    b = r % 2
                    gk = ("xqs", c * 4 + r - 2)
                    md = P.do("sync", lambda e, r_=r, b_=b, c_=c: e.dma_start(
                        XQS.rearrange("p (b c) -> p b c", c=1024)[:, b_],
                        sc_xq.rearrange("(s p) c -> s p c", p=128)[4 * c_ + r_]),
                        waits=pbf + [shared[("xqdma", 3)]]
                        + ([pcg[gk]] if gk in pcg else []), dma="xs" + str(b))
                    pm = None
                    for i in range(8):
                        wz = [md] + (wb(4 + r % 2) if i == 0 else [])
                        pm = P.do("tensor", lambda e, i_=i, b_=b, r_=r,
                                  a_=B45[r % 2]: e.transpose(
                                      a_.rearrange("p (i c) -> p i c", c=128)[:, i_],
                                      XQS.rearrange("p (b c) -> p b c", c=1024)[:, b_]
                                      .rearrange("p (i c) -> p i c", c=128)[:, i_],
                                      IDENT[:, :]), waits=wz)
                    pcg[("xqs", c * 4 + r)] = pm
                    me = P.do("vector", lambda e, r_=r, a_=B45[r % 2]: e.tensor_copy(
                        XQTT.rearrange("p (i c) -> p i c", c=512)
                        [:, :, r_ * 128:(r_ + 1) * 128],
                        a_.rearrange("p (i c) -> p i c", c=128)),
                        waits=[pm])
                    pfree[bkey(4 + r % 2)] = me
                    mtr.append(me)
                pcg[("mtr", c)] = mtr

            def emit_A1(g):
                c, r = divmod(g, 4)
                b = r % 2
                SGp = [SG, SG2][g % 2]
                mtr = pcg[("mtr", c)]
                gky = ("y1s", g - 2)
                my1 = P.do("gpsimd", lambda e, r_=r, b_=b, c_=c: e.dma_start(
                    Y1S.rearrange("p (b c) -> p b c", c=1024)[:, b_],
                    sc_y1.rearrange("(s p) c -> s p c", p=128)[4 * c_ + r_]),
                    waits=pbf + [shared["y1_last"]]
                    + ([pcg[gky]] if gky in pcg else []), dma="ys" + str(b))
                hfw = pcg.get(("hflast", g - 1), [])
                hm = []
                for cc in range(6):
                    L = 512 if cc < 5 else 256
                    pg = None
                    pu = None
                    for i in range(8):
                        wz = [mwg, mtr[r]] + (wb(cc % 2) if i == 0 else [])
                        pg = P.do("tensor", lambda e, i_=i, cc_=cc, r_=r, L_=L,
                                  bk_=cc % 2:
                                  e.matmul(
                                      bank_ap(bk_)[:, 0:L_],
                                      XQTT[:, i_ * 512 + r_ * 128:
                                           i_ * 512 + r_ * 128 + 128],
                                      WG[:, i_ * DFF + cc_ * 512:
                                         i_ * DFF + cc_ * 512 + L_],
                                      start=(i_ == 0), stop=(i_ == 7),
                                      skip_group_check=True), waits=wz)
                    msl = P.do("scalar", lambda e, cc_=cc, L_=L, bk_=cc % 2,
                               s_=SGp: e.activation(
                                   s_[:, cc_ * 512: cc_ * 512 + L_],
                                   bank_ap(bk_)[:, 0:L_], AF.Silu),
                               waits=[pg]
                               + ([pcg[("sgu", g - 2)]]
                                  if ("sgu", g - 2) in pcg else []))
                    pfree[cc % 2] = msl
                    for i in range(8):
                        wz = [mwu, mtr[r]] + (wb(2 + cc % 2) if i == 0 else [])
                        pu = P.do("tensor", lambda e, i_=i, cc_=cc, r_=r, L_=L,
                                  bk_=2 + cc % 2:
                                  e.matmul(
                                      bank_ap(bk_)[:, 0:L_],
                                      XQTT[:, i_ * 512 + r_ * 128:
                                           i_ * 512 + r_ * 128 + 128],
                                      WU[:, i_ * DFF + cc_ * 512:
                                         i_ * DFF + cc_ * 512 + L_],
                                      start=(i_ == 0), stop=(i_ == 7),
                                      skip_group_check=True), waits=wz)
                    mue = P.do("scalar", lambda e, cc_=cc, L_=L, bk_=2 + cc % 2:
                               e.activation(
                                   HF[:, cc_ * 512: cc_ * 512 + L_],
                                   bank_ap(bk_)[:, 0:L_], AF.Identity,
                                   bias=SM_ZERO), waits=[pu] + hfw)
                    pfree[2 + cc % 2] = mue
                    mh = P.do("vector", lambda e, cc_=cc, L_=L, s_=SGp:
                              e.tensor_tensor(
                                  HF[:, cc_ * 512: cc_ * 512 + L_],
                                  HF[:, cc_ * 512: cc_ * 512 + L_],
                                  s_[:, cc_ * 512: cc_ * 512 + L_], op=ALU.mult),
                              waits=[mue, msl])
                    hm.append(mh)
                    mbr = P.do("vector", lambda e, cc_=cc, L_=L: e.tensor_reduce(
                        S2B[:, cc_ * 32: cc_ * 32 + L_ // 16],
                        HF[:, cc_ * 512: cc_ * 512 + L_]
                        .rearrange("p (a b) -> p a b", b=16),
                        axis=AX.X, op=ALU.max, apply_absolute_value=True),
                        waits=[mh])
                pcg[("hbr", g)] = mbr
                pcg[("hm", g)] = hm
                pcg[("my1", g)] = my1

            def emit_chain(g):
                SGp = [SG, SG2][g % 2]
                HQp = [HQ, HQ2][g % 2]
                hm = pcg.pop(("hm", g))
                msg1 = P.do("vector", lambda e, s_=SGp: e.tensor_scalar(
                    s_[:, :], HF[:, :], 0.0, -2.0, op0=ALU.is_lt, op1=ALU.mult),
                    waits=hm)
                msg2 = P.do("vector", lambda e, s_=SGp: e.tensor_scalar(
                    s_[:, :], s_[:, :], 1.0, None, op0=ALU.add), waits=[msg1])
                mb1 = pcg.pop(("hbr", g))
                mb2 = P.do("vector", lambda e: e.tensor_scalar(
                    S2_HS, S2B, C6, 1e-12, op0=ALU.mult, op1=ALU.max),
                    waits=[mb1])
                mb3a = P.do("scalar", lambda e: e.activation(
                    S2_HB, S2_HS, AF.Ln, bias=SM_ZERO), waits=[mb2])
                mb3 = P.do("scalar", lambda e: e.activation(
                    S2_HR, S2_HB, AF.Exp, scale=-1.0), waits=[mb3a])
                pcg["hrcp"] = mb3
                mu1 = P.do("vector", lambda e: e.tensor_scalar(
                    HF.bitcast(I32)[:, :], HF.bitcast(I32)[:, :], 0x7FFFFFFF,
                    None, op0=ALU.bitwise_and), waits=[msg1, mb1])
                mu2 = P.do("vector", lambda e: e.tensor_tensor(
                    HF.rearrange("p (a b) -> p a b", b=16),
                    HF.rearrange("p (a b) -> p a b", b=16),
                    S2_HR.unsqueeze(2).broadcast_to([128, 176, 16]),
                    op=ALU.mult), waits=[mu1, mb3])
                mf1 = P.do("vector", lambda e: e.tensor_scalar(
                    HQT2[:, :], HF[:, :], 64.0, None, op0=ALU.add),
                    waits=[mu2] + ([pcg[("hqt2", g - 1)]]
                                   if ("hqt2", g - 1) in pcg else []))
                ma1 = P.do("vector", lambda e: e.tensor_scalar(
                    HQT2[:, :], HQT2[:, :], 64.0, 2.0,
                    op0=ALU.subtract, op1=ALU.min), waits=[mf1])
                mf2 = P.do("vector", lambda e, h_=HQp: e.tensor_scalar(
                    h_[:, :], HF[:, :], 128.0, None, op0=ALU.add),
                    waits=[ma1] + pcg.get(("hqpe", g - 2), []))
                mb1_ = P.do("vector", lambda e, h_=HQp: e.tensor_scalar(
                    h_[:, :], h_[:, :], 130.0, 0.0, op0=ALU.subtract,
                    op1=ALU.max), waits=[mf2])
                mg0a = P.do("vector", lambda e, h_=HQp: e.tensor_scalar(
                    h_[:, :], h_[:, :], 2.0, None, op0=ALU.min), waits=[mb1_])
                mg0 = P.do("vector", lambda e, h_=HQp: e.tensor_tensor(
                    h_[:, :], h_[:, :], HQT2[:, :], op=ALU.add), waits=[mg0a])
                mc1 = P.do("vector", lambda e: e.tensor_scalar(
                    HQT2[:, :], HF[:, :], 5.0, 2.0, op0=ALU.is_gt,
                    op1=ALU.mult), waits=[mg0])
                pcg[("hflast", g)] = [mc1]
                mg2 = P.do("vector", lambda e, h_=HQp: e.tensor_tensor(
                    h_[:, :], h_[:, :], HQT2[:, :], op=ALU.add),
                    waits=[mc1])
                pcg[("hqt2", g)] = mg2
                mt1 = P.do("vector", lambda e, s_=SGp, h_=HQp: e.tensor_tensor(
                    h_[:, :], h_[:, :], s_[:, :], op=ALU.mult),
                    waits=[mg2, msg2])
                pcg[("sgu", g)] = mt1
                mqf = P.do("vector", lambda e, h_=HQp: e.tensor_tensor(
                    h_.rearrange("p (a b) -> p a b", b=16),
                    h_.rearrange("p (a b) -> p a b", b=16),
                    S2_HS.unsqueeze(2).broadcast_to([128, 176, 16]),
                    op=ALU.mult), waits=[mt1])
                pcg[("mqf", g)] = mqf

            def emit_Bpe(g):
                HQp = [HQ, HQ2][g % 2]
                mqf = pcg.pop(("mqf", g))
                mtp = []
                lastpm = None
                rounds = [(0, 8, 4), (8, 8, 5), (16, 6, 4)]
                for (b0, nb_, bidx) in rounds:
                    bb_ap = PS8b[:, bidx * 1024:(bidx + 1) * 1024]
                    pm = None
                    for j in range(nb_):
                        wz = [mqf] + (wb(bidx) if j == 0 else [])
                        pm = P.do("tensor", lambda e, b_=b0 + j, j_=j, a_=bb_ap,
                                  h_=HQp:
                                  e.transpose(
                                      a_.rearrange("p (i c) -> p i c", c=128)[:, j_],
                                      h_.rearrange("p (i c) -> p i c", c=128)[:, b_],
                                      IDENT[:, :]), waits=wz)
                    me = P.do("vector", lambda e, b0_=b0, n_=nb_, a_=bb_ap:
                              e.tensor_copy(
                                  HQT[:, b0_ * 128:(b0_ + n_) * 128],
                                  a_[:, 0:n_ * 128]), waits=[pm])
                    pfree[bkey(bidx)] = me
                    mtp.append(me)
                    lastpm = pm
                pcg[("hqpe", g)] = [lastpm]
                pdw = {}
                for half in range(2):
                    pm = None
                    for j in range(22):
                        wz = [mwd, mtp[min(j // 8, 2)]] + (
                            wb(6 + half) if j == 0 else [])
                        pm = P.do("tensor", lambda e, j_=j, h_=half: e.matmul(
                            [G0, G1][h_][:, 0:512],
                            HQT[:, j_ * 128:(j_ + 1) * 128],
                            WD[:, j_ * D + h_ * 512: j_ * D + h_ * 512 + 512],
                            start=(j_ == 0), stop=(j_ == 21),
                            skip_group_check=True), waits=wz)
                    pdw[half] = pm
                pcg[("pdw", g)] = pdw

            def emit_Bfin(g):
                c, r = divmod(g, 4)
                b = r % 2
                pdw = pcg.pop(("pdw", g))
                my1 = pcg.pop(("my1", g))
                madd = None
                gko = ("ost", g - 2)
                for half in range(2):
                    madd = P.do("vector", lambda e, h_=half, b_=b: e.tensor_tensor(
                        OUTF.rearrange("p (b c) -> p b c", c=1024)
                        [:, b_, h_ * 512:(h_ + 1) * 512],
                        [G0, G1][h_][:, 0:512],
                        Y1S.rearrange("p (b c) -> p b c", c=1024)
                        [:, b_, h_ * 512:(h_ + 1) * 512], op=ALU.add),
                        waits=[pdw[half], my1]
                        + ([pcg[gko]] if gko in pcg else []))
                    pfree[bkey(6 + half)] = madd
                pcg[("y1s", g)] = madd
                yo = OUTF.rearrange("p (b c) -> p b c", c=1024)[:, b]
                ms = P.do("vector", lambda e, y_=yo: e.reduce_sum(
                    S2_R0, y_, axis=AX.X), waits=[madd])
                mq = P.do("vector", lambda e, y_=yo: e.scalar_tensor_tensor(
                    HQT[:, 0:1024], y_, 0.0, y_, op0=ALU.bypass, op1=ALU.mult,
                    accum_out=S2_R1), waits=[madd])
                mm_ = P.do("vector", lambda e: e.tensor_scalar(
                    S2_R2, S2_R0, 1.0 / 1024.0, None, op0=ALU.mult), waits=[ms])
                mv1 = P.do("vector", lambda e: e.tensor_scalar(
                    S2_R3, S2_R1, 1.0 / 1024.0, None, op0=ALU.mult), waits=[mq])
                mv2 = P.do("vector", lambda e: e.tensor_tensor(
                    S2_R4, S2_R2, S2_R2, op=ALU.mult), waits=[mm_])
                mv3 = P.do("vector", lambda e: e.tensor_tensor(
                    S2_R3, S2_R3, S2_R4, op=ALU.subtract), waits=[mv1, mv2])
                ml1 = P.do("scalar", lambda e: e.activation(
                    S2_R4, S2_R3, AF.Ln, bias=SM_EPS), waits=[mv3])
                ml2 = P.do("scalar", lambda e: e.activation(
                    S2_R3, S2_R4, AF.Exp, scale=-0.5), waits=[ml1])
                mn = P.do("vector", lambda e, y_=yo: e.tensor_scalar(
                    y_, y_, S2_R2, S2_R3, op0=ALU.subtract, op1=ALU.mult),
                    waits=[ml2])
                pcg[("ost", g)] = P.do(
                    "sync", lambda e, r_=r, y_=yo, c_=c: e.dma_start(
                        out_h.rearrange("(s p) c -> s p c", p=128)[4 * c_ + r_],
                        y_), waits=[mn], dma="out")
                pcg[("hqw", g)] = [mq]

            NG = 4 * n_tiles
            emit_xqst(0)
            for g in range(NG):
                if g >= 2:
                    emit_Bpe(g - 2)
                if g >= 1:
                    emit_chain(g - 1)
                if g >= 2:
                    emit_Bfin(g - 2)
                emit_A1(g)
                if g % 4 == 3 and g + 1 < NG:
                    emit_xqst((g + 1) // 4)
            emit_Bpe(NG - 2)
            emit_chain(NG - 1)
            emit_Bfin(NG - 2)
            emit_Bpe(NG - 1)
            emit_Bfin(NG - 1)
            P.do("sync", lambda e: e.nop(), waits=[("out", P.sems["out"][1])])

        @block.tensor
        def _(eng):
            P.emit("tensor", eng)

        @block.scalar
        def _(eng):
            P.emit("scalar", eng)

        @block.vector
        def _(eng):
            P.emit("vector", eng)

        @block.gpsimd
        def _(eng):
            P.emit("gpsimd", eng)

        @block.sync
        def _(eng):
            P.emit("sync", eng)

    return nc, tap_h


def kernel(**inputs):
    x = np.ascontiguousarray(np.asarray(inputs["L0_attn_x"], dtype=np.float32))
    common = {
        "wqkv": np.ascontiguousarray(inputs["L0_attn_Wqkv_w"], dtype=np.float32),
        "bqkv": np.ascontiguousarray(inputs["L0_attn_Wqkv_b"], dtype=np.float32),
        "wconv": np.ascontiguousarray(inputs["L0_attn_conv_w"], dtype=np.float32),
        "bconv": np.ascontiguousarray(inputs["L0_attn_conv_b"], dtype=np.float32),
        "wout": np.ascontiguousarray(inputs["L0_attn_out_w"], dtype=np.float32),
        "bout": np.ascontiguousarray(inputs["L0_attn_out_b"], dtype=np.float32),
        "wg": np.ascontiguousarray(inputs["L0_ffn_w_gate"], dtype=np.float32),
        "wu": np.ascontiguousarray(inputs["L0_ffn_w_up"], dtype=np.float32),
        "wd": np.ascontiguousarray(inputs["L0_ffn_w_down"], dtype=np.float32),
    }
    nc, _taps = build_program()
    in_maps = [dict(common, x=x[b]) for b in range(N_CORES)]
    res = run_bass_kernel_spmd(nc, in_maps, core_ids=list(range(N_CORES)))
    return np.stack([res.results[b]["out"] for b in range(N_CORES)], axis=0)


if __name__ == "__main__":
    nc, _ = build_program()
    n = {}
    for op in []:
        pass
    print("built OK")
